# version 03
# baseline (speedup 1.0000x reference)
import numpy as np
import ml_dtypes
from contextlib import ExitStack

import concourse.bass as bass
import concourse.mybir as mybir
from concourse.alu_op_type import AluOpType as OP
from concourse.bass_utils import run_bass_kernel_spmd

F32 = mybir.dt.float32
BF16 = mybir.dt.bfloat16
AF = mybir.ActivationFunctionType
AX = mybir.AxisListType
bfdt = ml_dtypes.bfloat16

B, S, H, NH = 4, 4096, 2048, 16
DN, DR, DV, QL, KVL = 128, 64, 128, 1536, 512
DQK = DN + DR
OWN = 2048
FF = 8192
SM = float(np.float32(DQK) ** np.float32(-0.5))
NEG = -1.0e9
EPS_RMS = 1e-6
EPS_LN = 1e-5
EPS_FFN = 1e-8

SCRATCH_KIND = "Internal"
DBG = False
PHASES = ("p0a", "p0b", "p1", "p2a", "p2b")


class Sched:
    ENG = ("sync", "vector", "scalar", "tensor", "gpsimd")
    COMPUTE_SEM = {"vector": "sV", "scalar": "sS", "tensor": "sT", "gpsimd": "sG"}

    def __init__(self, nc, es):
        self.nc = nc
        self._es = es
        self.q = {e: [] for e in self.ENG}
        self.sems = {}
        self.cnt = {}
        self.lastw = {}
        self.strict = set()
        self.last_ev = {}
        for s in ("dIn", "dSp", "dA", "dB", "sV", "sS", "sT", "sG"):
            self.sems[s] = es.enter_context(nc.semaphore(s))
            self.cnt[s] = 0

    def _wait(self, eng, dep):
        if dep is None:
            return
        s, v = dep
        if v <= 0:
            return
        key = (eng, s)
        if self.lastw.get(key, 0) >= v:
            return
        self.lastw[key] = v
        sem = self.sems[s]
        self.q[eng].append(lambda e, sem=sem, v=v: e.wait_ge(sem, v))

    CHAIN = ("vector", "scalar", "gpsimd")

    def add(self, eng, fn, deps=(), sem=None, amt=1):
        if eng in self.CHAIN:
            prev = self.last_ev.get(eng)
            if prev is not None:
                self._wait(eng, prev)
        for d in deps:
            self._wait(eng, d)
        if sem is None:
            self.q[eng].append(fn)
            return None
        h = self.sems[sem]
        self.cnt[sem] += amt
        v = self.cnt[sem]
        self.q[eng].append(lambda e, fn=fn, h=h, amt=amt: fn(e).then_inc(h, amt))
        if eng in self.CHAIN:
            self.last_ev[eng] = (sem, v)
        return (sem, v)

    def stream_sem(self, name):
        if name not in self.sems:
            self.sems[name] = self._es.enter_context(self.nc.semaphore(name))
            self.cnt[name] = 0
        return name

    def dma(self, eng, qsem, out_ap, in_ap, deps=(), slow=False):
        qsem = self.stream_sem(qsem)
        if slow:
            return self.add(eng, lambda e, o=out_ap, i=in_ap:
                            e.dma_start(out=o, in_=i, allow_slow_non_contiguous=True),
                            deps, qsem, 16)
        return self.add(eng, lambda e, o=out_ap, i=in_ap: e.dma_start(out=o, in_=i),
                        deps, qsem, 16)

    def te(self, fn, deps=()):
        return self.add("tensor", fn, deps, "sT")

    def ve(self, fn, deps=()):
        return self.add("vector", fn, deps, "sV")

    def ve_strict(self, fn, deps=()):
        ev = self.add("vector", fn, deps, "sV")
        self.strict.add(ev)
        return ev

    def sc(self, fn, deps=()):
        return self.add("scalar", fn, deps, "sS")

    def gp(self, fn, deps=()):
        return self.add("gpsimd", fn, deps, "sG")

    def barrier(self):
        evs = [(s, self.cnt[s]) for s in self.cnt]
        for e in self.ENG:
            for ev in evs:
                self._wait(e, ev)


def build_nc():
    nc = bass.Bass("TRN2", target_bir_lowering=False, debug=False,
                   detect_race_conditions=False)
    es = ExitStack()
    S_ = Sched(nc, es)

    def dram(name, shape, dt, kind="ExternalInput"):
        return nc.dram_tensor(name, shape, dt, kind=kind).ap()

    x_full = dram("x_full", [S, H], F32)
    x_own = dram("x_own", [OWN, H], F32)
    wqa_d = dram("wqa", [H, QL], F32)
    wqb_d = dram("wqb", [QL, NH * DQK], F32)
    wkva_d = dram("wkva", [H, KVL + DR], F32)
    wkvb_d = dram("wkvb", [KVL, NH * (DN + DV)], F32)
    wo_d = dram("wo", [H, H], F32)
    wg_d = dram("wg_t", [H, FF], F32)
    wu_d = dram("wu_t", [H, FF], F32)
    wd_d = dram("wd_t", [FF, H], F32)
    wlnq_d = dram("wlnq", [QL], F32)
    wlnkv_d = dram("wlnkv", [KVL], F32)
    cosk_d = dram("cosk", [S, DR], F32)
    srotk_d = dram("srotk", [S, DR], F32)
    cosq8_d = dram("cosq8", [OWN, 512], F32)
    srotq8_d = dram("srotq8", [OWN, 512], F32)
    masks_d = dram("masks", [8, 128, 512], BF16)
    idf_d = dram("idf", [128, 128], F32)
    idb_d = dram("idb", [128, 128], BF16)
    onescb_d = dram("onescb", [128, 1], BF16)
    onesrf_d = dram("onesrf", [1, 128], F32)
    heye_d = dram("heye", [NH, 128, NH], F32)

    y_d = dram("y", [OWN, H], F32, kind="ExternalOutput")

    SK = SCRATCH_KIND
    sgg_d = dram("sg_g", [H, FF], BF16, kind=SK)
    sgu_d = dram("sg_u", [H, FF], BF16, kind=SK)
    sgd_d = dram("sg_d", [FF, H], BF16, kind=SK)
    knh_d = dram("kn_h", [NH, 128, S], BF16, kind=SK)
    knl_d = dram("kn_l", [NH, 128, S], BF16, kind=SK)
    vth_d = dram("vt_h", [NH, 128, S], BF16, kind=SK)
    vtl_d = dram("vt_l", [NH, 128, S], BF16, kind=SK)
    qah_d = dram("qa_h", [12, 128, OWN], BF16, kind=SK)
    qal_d = dram("qa_l", [12, 128, OWN], BF16, kind=SK)
    qnh_d = dram("qn_h", [NH, 128, OWN], BF16, kind=SK)
    qnl_d = dram("qn_l", [NH, 128, OWN], BF16, kind=SK)
    qph_d = dram("qp_h", [NH, DR + 1, OWN], BF16, kind=SK)
    qpl_d = dram("qp_l", [NH, DR + 1, OWN], BF16, kind=SK)
    ath_d = dram("at_h", [NH, 128, OWN], BF16, kind=SK)
    atl_d = dram("at_l", [NH, 128, OWN], BF16, kind=SK)
    z_d = dram("z_sp", [16, 128, H], F32, kind=SK)
    if DBG:
        dbg_ckvh = dram("dbg_ckvh", [128, 4 * S], BF16, kind="ExternalOutput")
        dbg_ckvl = dram("dbg_ckvl", [128, 4 * S], BF16, kind="ExternalOutput")
        dbg_qa32 = dram("dbg_qa32", [128, 4 * QL], F32, kind="ExternalOutput")
        dbg_nrm = dram("dbg_nrm", [128, 32], F32, kind="ExternalOutput")
        dbg_qpeh = dram("dbg_qpeh", [128, 512], BF16, kind="ExternalOutput")
        dbg_qn2 = dram("dbg_qn2", [NH, OWN], F32, kind="ExternalOutput")
        dbg_qr = dram("dbg_qr", [128, 12 * 1024], BF16, kind="ExternalOutput")
        dbg_qrl = dram("dbg_qrl", [128, 12 * 1024], BF16, kind="ExternalOutput")
        dbg_kn2 = dram("dbg_kn2", [NH, S], F32, kind="ExternalOutput")
        dbg_csc = dram("dbg_csc", [NH, 1], F32, kind="ExternalOutput")
        dbg_qpx = dram("dbg_qpx", [65, 1024], BF16, kind="ExternalOutput")
        dbg_rsq = dram("dbg_rsq", [128, 32], F32, kind="ExternalOutput")
        dbg_xq32 = dram("dbg_xq32", [128, 4 * H], F32, kind="ExternalOutput")
    xq_d = dram("xq_sp", [128, 16 * OWN], BF16, kind=SK)

    def sb(name, shape, dt):
        return es.enter_context(nc.sbuf_tensor(name, shape, dt))

    pq = es.enter_context(nc.psum_tensor("pq", [128, 2048], F32))
    pr = es.enter_context(nc.psum_tensor("pr", [128, 1024], F32))
    pf0 = es.enter_context(nc.psum_tensor("pf0", [128, 1024], BF16))
    pf1 = es.enter_context(nc.psum_tensor("pf1", [128, 1024], BF16))
    pf0f = pf0[:, :].bitcast(F32)
    pf1f = pf1[:, :].bitcast(F32)

    idf = sb("idf_s", [128, 128], F32)
    idb = sb("idb_s", [128, 128], BF16)
    onescb = sb("onescb_s", [128, 1], BF16)
    onesrf = sb("onesrf_s", [1, 128], F32)
    heye = sb("heye_s", [128, NH * NH], F32)
    csc = sb("csc", [NH, 1], F32)
    rs_all = sb("rs_all", [128, 16], F32)

    ev_c = None
    for t, d in ((idf, idf_d), (idb, idb_d), (onescb, onescb_d), (onesrf, onesrf_d)):
        ev_c = S_.dma("sync", "c0", t[:], d)
    for h in range(NH):
        ev_c = S_.dma("sync", "c0", heye[:, h * NH:(h + 1) * NH], heye_d[h])
    ev_consts = ev_c

    def hilo(f32_ap, h_ap, l_ap, deps, hfree=None, lfree=None):
        evh = S_.ve(lambda e, o=h_ap, i=f32_ap: e.tensor_copy(o, i),
                    list(deps) + [hfree])
        evl = S_.gp(lambda e, o=l_ap, a=f32_ap, b=h_ap: e.tensor_tensor(o, a, b, OP.subtract),
                    [evh, lfree] + list(deps))
        return evh, evl

    def mm3(psum_ap, lh, ll, rh, rl, first, last, deps=()):
        S_.te(lambda e, o=psum_ap, a=lh, b=rh, s=first:
              e.matmul(o, a, b, start=s, stop=False), deps)
        S_.te(lambda e, o=psum_ap, a=lh, b=rl:
              e.matmul(o, a, b, start=False, stop=False))
        return S_.te(lambda e, o=psum_ap, a=ll, b=rh, p=last:
                     e.matmul(o, a, b, start=False, stop=p))

    def trans8(srcs, bank_part_rows, bank, ident, deps):
        ev = None
        n = len(srcs)
        for i, a in enumerate(srcs):
            w = a.shape[1]
            o = bank[0:w, i * 128:(i + 1) * 128]
            ev = S_.te(lambda e, o=o, a=a, r=ident, s=(i == 0), p=(i == n - 1):
                       e.matmul(o, a, r, is_transpose=True, start=s, stop=p),
                       deps if i == 0 else ())
        return ev

    sgn_jobs = []
    for src, dst, K, ncols in ((wg_d, sgg_d, H, FF), (wu_d, sgu_d, H, FF),
                               (wd_d, sgd_d, FF, H)):
        for r0 in range(0, K, 128):
            for c0 in range(0, ncols, 512):
                sgn_jobs.append((src, dst, r0, c0))
    sgin = [sb(f"sgin{i}", [128, 512], F32) for i in range(2)]
    sgout = [sb(f"sgout{i}", [128, 512], BF16) for i in range(2)]
    sg_state = {"i": 0, "fin": [None, None], "fout": [None, None]}
    kpe_scope = ExitStack()
    kpeh = kpe_scope.enter_context(nc.sbuf_tensor("kpeh", [DR + 1, S], BF16))
    kpel = kpe_scope.enter_context(nc.sbuf_tensor("kpel", [DR + 1, S], BF16))

    def emit_sign_chunk(k=1):
        for _ in range(k):
            i = sg_state["i"]
            if i >= len(sgn_jobs):
                return
            sg_state["i"] = i + 1
            src, dst, r0, c0 = sgn_jobs[i]
            sl = i % 2
            evi = S_.dma("gpsimd", f"sgi{sl}", sgin[sl][:], src[r0:r0 + 128, c0:c0 + 512],
                         [sg_state["fin"][sl]])
            evs = S_.sc(lambda e, o=sgout[sl][:], x=sgin[sl][:]: e.activation(o, x, AF.Sign),
                        [evi, sg_state["fout"][sl]])
            evo = S_.dma("gpsimd", f"sgo{sl}", dst[r0:r0 + 128, c0:c0 + 512], sgout[sl][:], [evs])
            sg_state["fin"][sl] = evs
            sg_state["fout"][sl] = evo

    p0 = ExitStack()

    def sb0(n, s, d):
        return p0.enter_context(nc.sbuf_tensor(n, s, d))

    ckvh = sb0("ckvh", [128, 4 * S], BF16)
    ckvl = sb0("ckvl", [128, 4 * S], BF16)
    kpn = sb0("kpn", [128, 32], F32)
    knorm2 = sb0("knorm2", [NH, S], F32)
    kpnT = sb0("kpnT", [32, 128], F32)
    mx1 = sb0("mx1", [32, 1], F32)
    mx2 = sb0("mx2", [1, 32], F32)
    mkp2 = sb0("mkp2", [1, 1], F32)
    pkva = ExitStack()

    def sbk(n, s, d):
        return pkva.enter_context(nc.sbuf_tensor(n, s, d))

    xt = [sbk(f"xt{i}", [128, H], F32) for i in range(2)]
    xf32s = sbk("xf32s", [128, H], F32)
    xf32 = [xf32s, xf32s]
    xh = [sbk(f"xh{i}", [128, H], BF16) for i in range(2)]
    xl = [sbk(f"xl{i}", [128, H], BF16) for i in range(2)]
    wkvaF = sbk("wkvaF", [128, 576], F32)
    wkvah = sbk("wkvah", [128, 16 * 576], BF16)
    wkval = sbk("wkval", [128, 16 * 576], BF16)
    sqd = sbk("sqd", [128, 576], F32)
    nrm = sbk("nrm", [128, 1], F32)
    rsq = sbk("rsq", [128, 1], F32)
    chat32 = sbk("chat32", [128, 512], F32)
    chath = sbk("chath", [128, 512], BF16)
    chatl = sbk("chatl", [128, 512], BF16)
    wlnkv = sbk("wlnkv_s", [128, 4], F32)
    costl = [sbk(f"costl{i}", [128, DR], F32) for i in range(2)]
    srotl = [sbk(f"srotl{i}", [128, DR], F32) for i in range(2)]
    kp32 = sbk("kp32", [128, DR], F32)
    kp32b = sbk("kp32b", [128, DR], F32)
    kpstg = sbk("kpstg", [DR, 128], F32)

    ev_wln = S_.dma("sync", "wln0", wlnkv[:], wlnkv_d.rearrange("(c p) -> p c", p=128), slow=True)

    prevl = None
    for kc in range(16):
        evd = S_.dma("sync", "wkva", wkvaF[:], wkva_d[kc * 128:(kc + 1) * 128, :], [prevl])
        evh, evl = hilo(wkvaF[:], wkvah[:, kc * 576:(kc + 1) * 576],
                        wkval[:, kc * 576:(kc + 1) * 576], [evd])
        prevl = evl
    ev_wkva_last = prevl

    ck_sets = [(pq[:, 0:512], pq[:, 512:576]), (pq[:, 1024:1536], pq[:, 1536:1600])]
    ck_free = [None, None]
    ck_free2 = [None, None]
    tr_free = [None, None]
    xt_free = [None, None]
    xf32_free = None
    xh_free = [None, None]
    kpstg_free = None
    ev_ck_ready = None
    ev_rope_done = [None, None]
    chat_free = None
    ckvh3 = ckvh[:].rearrange("p (c t) -> p c t", t=S)
    ckvl3 = ckvl[:].rearrange("p (c t) -> p c t", t=S)
    pf0h3 = pf0[:, 0:512].rearrange("p (c t) -> p c t", t=128)
    pf0l3 = pf0[:, 512:1024].rearrange("p (c t) -> p c t", t=128)

    for tc in range(32):
        sl = tc % 2
        emit_sign_chunk(12)
        evx = S_.dma("sync", f"xt{sl}", xt[sl][:], x_full[tc * 128:(tc + 1) * 128, :],
                     [xt_free[sl]])
        evc = S_.dma("sync", f"co{sl}", costl[sl][:], cosk_d[tc * 128:(tc + 1) * 128, :],
                     [ev_rope_done[sl]])
        evsr = S_.dma("sync", f"sr{sl}", srotl[sl][:], srotk_d[tc * 128:(tc + 1) * 128, :],
                      [ev_rope_done[sl]])
        last_ev = None
        for g in range(4):
            bank = pr[:, (g % 2) * 512:(g % 2) * 512 + 512]
            evt = trans8([xt[sl][:, (g * 4 + j) * 128:(g * 4 + j + 1) * 128]
                          for j in range(4)], 128, bank, idf[:],
                         [evx, ev_consts, tr_free[g % 2]])
            last_ev = S_.sc(lambda e, o=xf32[sl][:, g * 512:(g + 1) * 512], i=bank:
                            e.copy(o, i), [evt, xf32_free])
            tr_free[g % 2] = last_ev
        xt_free[sl] = last_ev
        evh, evl = hilo(xf32[sl][:], xh[sl][:], xl[sl][:], [last_ev],
                        hfree=xh_free[sl], lfree=xh_free[sl])
        xf32_free = evl

        cs = tc % 2
        pA, pB = ck_sets[cs]
        ev_mmA = ev_mmB = None
        for os_, pap, w in ((0, pA, 512), (1, pB, 64)):
            last_mm = None
            for kc in range(16):
                deps = ([evl, ev_wkva_last, ck_free[cs], ck_free2[cs]] if kc == 0 else ())
                last_mm = mm3(pap,
                              xh[sl][:, kc * 128:(kc + 1) * 128],
                              xl[sl][:, kc * 128:(kc + 1) * 128],
                              wkvah[:, kc * 576 + os_ * 512: kc * 576 + os_ * 512 + w],
                              wkval[:, kc * 576 + os_ * 512: kc * 576 + os_ * 512 + w],
                              first=(kc == 0), last=(kc == 15), deps=deps)
            if os_ == 0:
                ev_mmA = last_mm
            else:
                ev_mmB = last_mm
        xh_free[sl] = ev_mmB

        evsq = S_.sc(lambda e, i=pA: e.activation(sqd[:, 0:512], i, AF.Square,
                                                  accum_out=nrm[:]), [ev_mmA])
        ev1 = S_.ve(lambda e: e.tensor_scalar(rsq[:], nrm[:], 1.0 / KVL, EPS_RMS,
                                              OP.mult, OP.add), [evsq])
        ev2 = S_.sc(lambda e: e.activation(rsq[:], rsq[:], AF.Sqrt), [ev1])
        ev3 = S_.ve_strict(lambda e: e.reciprocal(rsq[:], rsq[:]), [ev2])
        if DBG:
            S_.dma("sync", "dbg", dbg_nrm[:, tc:tc + 1], nrm[:], [ev1], slow=True)
            S_.dma("sync", "dbg", dbg_rsq[:, tc:tc + 1], rsq[:], [ev3], slow=True)
        evch = S_.ve(lambda e, i=pA: e.tensor_scalar(chat32[:], i, rsq[:], None, OP.mult),
                     [ev3, chat_free])
        evh2, evl2 = hilo(chat32[:], chath[:], chatl[:], [evch])
        evtr = trans8([chath[:, j * 128:(j + 1) * 128] for j in range(4)] +
                      [chatl[:, j * 128:(j + 1) * 128] for j in range(4)],
                      128, pf0[:, :], idb[:], [evh2, evl2, chat_free])
        evcka = S_.sc(lambda e, o=ckvh3[:, 0:4, tc * 128:(tc + 1) * 128], i=pf0h3:
                      e.copy(o, i), [evtr])
        evckb = S_.sc(lambda e, o=ckvl3[:, 0:4, tc * 128:(tc + 1) * 128], i=pf0l3:
                      e.copy(o, i), [evtr])
        ev_ck_ready = evckb
        chat_free = evckb

        evr1 = S_.ve(lambda e, i=pB, c=costl[sl]: e.tensor_tensor(kp32[:], i, c[:], OP.mult),
                     [ev_mmB, evc])
        evr2 = S_.ve(lambda e, i=pB, sr=srotl[sl]:
                     e.tensor_tensor(kp32b[:, 0:32], i[:, 32:64], sr[:, 0:32], OP.mult),
                     [evsr])
        evr3 = S_.ve(lambda e, i=pB, sr=srotl[sl]:
                     e.tensor_tensor(kp32b[:, 32:64], i[:, 0:32], sr[:, 32:64], OP.mult), [])
        evr4 = S_.ve(lambda e: e.tensor_tensor(kp32[:], kp32[:], kp32b[:], OP.add), [])
        evkn = S_.sc(lambda e, a=kpn[:, tc:tc + 1]:
                     e.activation(sqd[:, 0:64], kp32[:], AF.Square, accum_out=a), [evr4])
        evtp = S_.te(lambda e, o=pf1f[0:64, 0:128]:
                     e.matmul(o, kp32[:], idf[:], is_transpose=True, start=True, stop=True),
                     [evr4, evckb])
        evke = S_.sc(lambda e, i=pf1f[0:64, 0:128]: e.copy(kpstg[:], i),
                     [evtp, kpstg_free])
        evkh_, evkl_ = hilo(kpstg[:], kpeh[0:64, tc * 128:(tc + 1) * 128],
                            kpel[0:64, tc * 128:(tc + 1) * 128], [evke])
        kpstg_free = evkl_
        ev_rope_done[sl] = evke
        ck_free[cs] = evke
        ck_free2[cs] = evl2

    S_.ve(lambda e: e.memset(kpeh[64:65, :], -1.0), [])
    S_.ve(lambda e: e.memset(kpel[64:65, :], 0.0), [])

    evq1 = S_.te(lambda e, o=pf1f[0:32, 0:128]:
                 e.matmul(o, kpn[:], idf[:], is_transpose=True, start=True, stop=True),
                 [ev_rope_done[0], ev_rope_done[1]])
    evq2 = S_.ve(lambda e, i=pf1f[0:32, 0:128]: e.tensor_copy(kpnT[:], i), [evq1])
    evq3 = S_.ve_strict(lambda e: e.reduce_max(mx1[:], kpnT[:], axis=AX.X), [evq2])
    evq4 = S_.te(lambda e, o=pf1f[0:1, 0:32]:
                 e.matmul(o, mx1[:], idf[0:32, 0:32], is_transpose=True,
                          start=True, stop=True), [evq3])
    evq5 = S_.ve(lambda e, i=pf1f[0:1, 0:32]: e.tensor_copy(mx2[:], i), [evq4])
    evq6 = S_.ve_strict(lambda e: e.reduce_max(mkp2[:], mx2[:], axis=AX.X), [evq5])

    if DBG:
        S_.barrier()
        S_.dma("sync", "dbg", dbg_ckvh[:, :], ckvh[:])
        S_.dma("sync", "dbg", dbg_ckvl[:, :], ckvl[:])

    S_.barrier()
    pkva.close()
    wkvbF = sb0("wkvbF", [128, 2048], F32)
    wkvbh = sb0("wkvbh", [128, 4 * 4096], BF16)
    wkvbl = sb0("wkvbl", [128, 4 * 4096], BF16)
    wlnkv2 = sb0("wlnkv2", [128, 4], F32)
    ev_wln2 = S_.dma("sync", "wln1", wlnkv2[:], wlnkv_d.rearrange("(c p) -> p c", p=128), slow=True)
    prevl = None
    for kc in range(4):
        for hf in range(2):
            evd = S_.dma("sync", "wkvb", wkvbF[:],
                         wkvb_d[kc * 128:(kc + 1) * 128, hf * 2048:(hf + 1) * 2048],
                         [prevl])
            evsc = S_.ve(lambda e, kc=kc: e.tensor_scalar(wkvbF[:], wkvbF[:],
                                                          wlnkv2[:, kc:kc + 1], None,
                                                          OP.mult), [evd, ev_wln2])
            evh3, evl3 = hilo(wkvbF[:],
                              wkvbh[:, kc * 4096 + hf * 2048: kc * 4096 + (hf + 1) * 2048],
                              wkvbl[:, kc * 4096 + hf * 2048: kc * 4096 + (hf + 1) * 2048],
                              [evsc])
            prevl = evl3
    ev_wkvb_last = prevl

    kn32 = [sb0(f"kn32_{i}", [128, 512], F32) for i in range(2)]
    knh_s = [sb0(f"knh_s{i}", [128, 512], BF16) for i in range(2)]
    knl_s = [sb0(f"knl_s{i}", [128, 512], BF16) for i in range(2)]
    vth_st = [sb0(f"vth_st{i}", [128, 512], BF16) for i in range(2)]
    vtl_st = [sb0(f"vtl_st{i}", [128, 512], BF16) for i in range(2)]
    sq2 = sb0("sq2", [128, 512], F32)

    ckv_all = ev_ck_ready
    kv_banks = [pq[:, 0:512], pq[:, 512:1024], pq[:, 1024:1536],
                pq[:, 1536:2048], pr[:, 0:512], pr[:, 512:1024]]
    bank_free = [[ck_free[0]], [ck_free[0]], [ck_free[1]], [ck_free[1]],
                 [tr_free[0]], [tr_free[1]]]
    st_free = [None, None]
    sq2_free = None
    pf0_free = ckv_all
    pf1_free = evq6
    evkn2_prev = None
    knorm_init = S_.ve(lambda e: e.memset(knorm2[:], 0.0), [])

    for g0, g1 in ((0, 3), (3, 6), (6, 8)):
        nts = g1 - g0
        for oc in range(32):
            emit_sign_chunk(2)
            bsel = [(oc % 2) * 3 + i for i in range(nts)]
            last_mm = [None] * nts
            for i, ts in enumerate(range(g0, g1)):
                pap = kv_banks[bsel[i]]
                for kc in range(4):
                    deps = (([ev_wkvb_last, ckv_all] + list(bank_free[bsel[i]] or []))
                            if kc == 0 else ())
                    last_mm[i] = mm3(
                        pap,
                        wkvbh[:, kc * 4096 + oc * 128: kc * 4096 + (oc + 1) * 128],
                        wkvbl[:, kc * 4096 + oc * 128: kc * 4096 + (oc + 1) * 128],
                        ckvh3[:, kc, ts * 512:(ts + 1) * 512],
                        ckvl3[:, kc, ts * 512:(ts + 1) * 512],
                        first=(kc == 0), last=(kc == 3), deps=deps)
            for i, ts in enumerate(range(g0, g1)):
                pap = kv_banks[bsel[i]]
                sl2 = (oc * nts + i) % 2
                ev32 = S_.sc(lambda e, i2=pap, o=kn32[sl2]: e.copy(o[:], i2),
                             [last_mm[i], st_free[sl2]])
                if oc < 16:
                    h = oc
                    evsqr = S_.sc(lambda e, i2=pap: e.activation(sq2[:], i2, AF.Square),
                                  [last_mm[i], sq2_free])
                    evn = S_.te(lambda e, o=pf1f[0:16, 0:512],
                                a=heye[:, h * NH:(h + 1) * NH]:
                                e.matmul(o, a, sq2[:], start=True, stop=True),
                                [evsqr, pf1_free])
                    sq2_free = evn
                    evkn2 = S_.ve(lambda e, o=knorm2[:, ts * 512:(ts + 1) * 512],
                                  i2=pf1f[0:16, 0:512]: e.tensor_tensor(o, o, i2, OP.add),
                                  [evn, knorm_init, evkn2_prev])
                    pf1_free = evkn2
                    evkn2_prev = evkn2
                    evh4, evl4 = hilo(kn32[sl2][:], knh_s[sl2][:], knl_s[sl2][:], [ev32],
                                      hfree=st_free[sl2], lfree=st_free[sl2])
                    e1 = S_.dma("scalar", f"spkn{sl2}", knh_d[h][:, ts * 512:(ts + 1) * 512],
                                knh_s[sl2][:], [evh4])
                    e2 = S_.dma("scalar", f"spkn{sl2}", knl_d[h][:, ts * 512:(ts + 1) * 512],
                                knl_s[sl2][:], [evl4])
                    st_free[sl2] = e2
                    bank_free[bsel[i]] = [evl4, evn]
                else:
                    h = oc - 16
                    evh5, evl5 = hilo(kn32[sl2][:], knh_s[sl2][:], knl_s[sl2][:], [ev32],
                                      hfree=st_free[sl2], lfree=st_free[sl2])
                    bank_free[bsel[i]] = [evl5]
                    evtr2 = trans8([knh_s[sl2][:, j * 128:(j + 1) * 128] for j in range(4)] +
                                   [knl_s[sl2][:, j * 128:(j + 1) * 128] for j in range(4)],
                                   128, pf0[:, :], idb[:], [evh5, evl5, pf0_free])
                    evv1 = S_.sc(lambda e, o=vth_st[sl2][:], i2=pf0[:, 0:512]:
                                 e.copy(o, i2), [evtr2, st_free[sl2]])
                    evv2 = S_.sc(lambda e, o=vtl_st[sl2][:], i2=pf0[:, 512:1024]:
                                 e.copy(o, i2), [evtr2])
                    pf0_free = evv2
                    e3 = S_.dma("scalar", f"spv{sl2}", vth_d[h][:, ts * 512:(ts + 1) * 512],
                                vth_st[sl2][:], [evv1])
                    e4 = S_.dma("scalar", f"spv{sl2}", vtl_d[h][:, ts * 512:(ts + 1) * 512],
                                vtl_st[sl2][:], [evv2])
                    st_free[sl2] = e4

    if DBG:
        S_.dma("sync", "dbg", dbg_kn2[:, :], knorm2[:], [evkn2_prev])
    knmax = sb0("knmax", [NH, 1], F32)
    evm = S_.ve_strict(lambda e: e.reduce_max(knmax[:], knorm2[:], axis=AX.X), [evkn2_prev])
    evrep = S_.te(lambda e, o=pf1f[0:16, 0:1]:
                  e.matmul(o, onesrf[0:1, 0:16], mkp2[:], start=True, stop=True),
                  [evq6, ev_consts, pf1_free])
    evmk = S_.ve(lambda e, i=pf1f[0:16, 0:1]: e.tensor_tensor(csc[:], knmax[:], i, OP.add),
                 [evm, evrep])
    evmk2 = S_.sc(lambda e: e.activation(csc[:], csc[:], AF.Sqrt), [evmk])

    S_.barrier()
    p0.close()

    pb_ = ExitStack()

    def sb1(n, s, d):
        return pb_.enter_context(nc.sbuf_tensor(n, s, d))

    wlnq = sb1("wlnq_s", [128, 12], F32)
    qnorm2 = sb1("qnorm2", [NH, OWN], F32)
    ev_wlq = S_.dma("sync", "wln2", wlnq[:], wlnq_d.rearrange("(c p) -> p c", p=128), slow=True)

    pqa = ExitStack()

    def sb1a(n, s, d):
        return pqa.enter_context(nc.sbuf_tensor(n, s, d))

    xtq = [sb1a(f"xtq{i}", [128, H], F32) for i in range(2)]
    xq32 = sb1a("xq32", [128, 4 * H], F32)
    xqh = sb1a("xqh", [128, 4 * H], BF16)
    xql = sb1a("xql", [128, 4 * H], BF16)
    wqaF = [sb1a(f"wqaF{i}", [128, 512], F32) for i in range(2)]
    wqah = [sb1a(f"wqah{i}", [128, 512], BF16) for i in range(2)]
    wqal = [sb1a(f"wqal{i}", [128, 512], BF16) for i in range(2)]
    qa32 = sb1a("qa32", [128, 4 * QL], F32)
    nqp = sb1a("nqp", [128, 12], F32)
    sqd1 = sb1a("sqd1", [128, 512], F32)
    nrm1 = sb1a("nrm1", [128, 1], F32)
    rsq1 = sb1a("rsq1", [128, 1], F32)
    qh32 = sb1a("qh32", [128, QL], F32)
    qhh = sb1a("qhh", [128, QL], BF16)
    qhl = sb1a("qhl", [128, QL], BF16)
    qafh = sb1a("qafh", [128, 12 * 512], BF16)
    qafl = sb1a("qafl", [128, 12 * 512], BF16)
    qafh3 = qafh[:].rearrange("p (c u) -> p c u", u=512)
    qafl3 = qafl[:].rearrange("p (c u) -> p c u", u=512)

    xt_free2 = [None, None]
    tr_free2 = [None, None]
    w_free = [None, None]
    bank_free3 = [None] * 4
    qaf_free = None
    ev_trans_last = None
    for qg in range(4):
        emit_sign_chunk(16)
        ev_ready = [None] * 4
        for t in range(4):
            tc = qg * 4 + t
            sl = tc % 2
            evx = S_.dma("sync", f"xtq{sl}", xtq[sl][:], x_own[tc * 128:(tc + 1) * 128, :],
                         [xt_free2[sl]])
            last_ev = None
            for g in range(4):
                bank = pr[:, (g % 2) * 512:(g % 2) * 512 + 512]
                evt = trans8([xtq[sl][:, (g * 4 + j) * 128:(g * 4 + j + 1) * 128]
                              for j in range(4)], 128, bank, idf[:],
                             [evx, ev_consts, tr_free2[g % 2]])
                last_ev = S_.sc(lambda e, o=xq32[:, t * H + g * 512: t * H + (g + 1) * 512],
                                i=bank: e.copy(o, i), [evt])
                tr_free2[g % 2] = last_ev
            xt_free2[sl] = last_ev
            evh, evl = hilo(xq32[:, t * H:(t + 1) * H], xqh[:, t * H:(t + 1) * H],
                            xql[:, t * H:(t + 1) * H], [last_ev])
            ev_ready[t] = evl
        for os_ in range(3):
            last_mm = [None] * 4
            for kc in range(16):
                wsl = kc % 2
                evd = S_.dma("sync", f"wqa{wsl}", wqaF[wsl][:],
                             wqa_d[kc * 128:(kc + 1) * 128, os_ * 512:(os_ + 1) * 512],
                             [w_free[wsl]])
                evh6, evl6 = hilo(wqaF[wsl][:], wqah[wsl][:], wqal[wsl][:], [evd],
                                  hfree=w_free[wsl], lfree=w_free[wsl])
                for t in range(4):
                    deps = ([ev_ready[t], evl6, bank_free3[t]] if kc == 0 else [evl6])
                    last_mm[t] = mm3(pq[:, t * 512:(t + 1) * 512],
                                     xqh[:, t * H + kc * 128: t * H + (kc + 1) * 128],
                                     xql[:, t * H + kc * 128: t * H + (kc + 1) * 128],
                                     wqah[wsl][:], wqal[wsl][:],
                                     first=(kc == 0), last=(kc == 15), deps=deps)
                w_free[wsl] = last_mm[3]
            for t in range(4):
                pap = pq[:, t * 512:(t + 1) * 512]
                evv = S_.sc(lambda e, i2=pap,
                            o=qa32[:, t * QL + os_ * 512: t * QL + (os_ + 1) * 512]:
                            e.copy(o, i2), [last_mm[t]])
                evs = S_.sc(lambda e, i2=pap, a=nqp[:, t * 3 + os_: t * 3 + os_ + 1]:
                            e.activation(sqd1[:], i2, AF.Square, accum_out=a),
                            [last_mm[t]])
                bank_free3[t] = evs
        for t in range(4):
            tc = qg * 4 + t
            ev1 = S_.ve_strict(lambda e, t=t: e.reduce_sum(nrm1[:], nqp[:, t * 3:t * 3 + 3],
                                                    axis=AX.X), [bank_free3[t]])
            ev2 = S_.ve(lambda e: e.tensor_scalar(rsq1[:], nrm1[:], 1.0 / QL, EPS_RMS,
                                                  OP.mult, OP.add), [ev1])
            ev3 = S_.sc(lambda e: e.activation(rsq1[:], rsq1[:], AF.Sqrt), [ev2])
            ev4 = S_.ve_strict(lambda e: e.reciprocal(rsq1[:], rsq1[:]), [ev3])
            ev5 = S_.ve(lambda e, t=t: e.tensor_scalar(qh32[:], qa32[:, t * QL:(t + 1) * QL],
                                                       rsq1[:], None, OP.mult),
                        [ev4, ev_trans_last])
            evh7, evl7 = hilo(qh32[:], qhh[:], qhl[:], [ev5], hfree=ev_trans_last,
                              lfree=ev_trans_last)
            tiles = [qhh[:, j * 128:(j + 1) * 128] for j in range(12)] + \
                    [qhl[:, j * 128:(j + 1) * 128] for j in range(12)]
            prev_evac = qaf_free
            for bi in range(3):
                evt = trans8(tiles[bi * 8:(bi + 1) * 8], 128, pf0[:, :], idb[:],
                             [evh7, evl7, prev_evac])
                ev_trans_last = evt
                for half in range(2):
                    idx0 = bi * 8 + half * 4
                    is_h = idx0 < 12
                    cc = idx0 if is_h else idx0 - 12
                    dst3 = qafh3 if is_h else qafl3
                    o = dst3[:, cc:cc + 4, t * 128:(t + 1) * 128]
                    i3 = (pf0h3 if half == 0 else pf0l3)
                    prev_evac = S_.sc(lambda e, o=o, i3=i3: e.copy(o, i3), [evt])
            qaf_free = prev_evac
        if DBG and qg == 0:
            S_.barrier()
            S_.dma("sync", "dbg", dbg_qa32[:, :], qa32[:])
            S_.dma("sync", "dbg", dbg_xq32[:, :], xq32[:])
            S_.barrier()
        spill_ev = None
        for cc in range(12):
            S_.dma("scalar", "spqa", qah_d[cc][:, qg * 512:(qg + 1) * 512],
                   qafh[:, cc * 512:(cc + 1) * 512], [qaf_free])
            spill_ev = S_.dma("scalar", "spqa", qal_d[cc][:, qg * 512:(qg + 1) * 512],
                              qafl[:, cc * 512:(cc + 1) * 512], [qaf_free])
        qaf_free = spill_ev

    S_.barrier()
    pqa.close()

    pqn = ExitStack()

    def sb1b(n, s, d):
        return pqn.enter_context(nc.sbuf_tensor(n, s, d))

    qnorm_init = S_.ve(lambda e: e.memset(qnorm2[:], 0.0), [])
    wqbF = [sb1b(f"wqbF{i}", [128, 256], F32) for i in range(2)]
    wqbh = [sb1b(f"wqbh{i}", [128, 256], BF16) for i in range(2)]
    wqbl = [sb1b(f"wqbl{i}", [128, 256], BF16) for i in range(2)]
    qrhA = sb1b("qrhA", [128, 12 * 1024], BF16)
    qrlA = sb1b("qrlA", [128, 12 * 1024], BF16)
    qn32 = [sb1b(f"qn32_{i}", [128, 512], F32) for i in range(2)]
    qnh_s = [sb1b(f"qnh_s{i}", [128, 512], BF16) for i in range(2)]
    qnl_s = [sb1b(f"qnl_s{i}", [128, 512], BF16) for i in range(2)]
    sq3 = sb1b("sq3", [128, 512], F32)

    w_free2 = [None, None]
    stg_free2 = [None, None]
    sq3_free = None
    pf1_free2 = None
    evqn_prev = None
    qr_free = None
    for tsg in range(2):
        ev_qr_last = None
        for kc in range(12):
            S_.dma("sync", "qr", qrhA[:, kc * 1024:(kc + 1) * 1024],
                   qah_d[kc][:, tsg * 1024:(tsg + 1) * 1024], [qr_free])
            ev_qr_last = S_.dma("sync", "qr", qrlA[:, kc * 1024:(kc + 1) * 1024],
                                qal_d[kc][:, tsg * 1024:(tsg + 1) * 1024], [qr_free])
        if DBG and tsg == 0:
            S_.dma("sync", "dbg", dbg_qr[:, :], qrhA[:], [ev_qr_last])
            S_.dma("sync", "dbg", dbg_qrl[:, :], qrlA[:], [ev_qr_last])
        bank_free4 = [None] * 4
        last_any = None
        for G in range(8):
            emit_sign_chunk(2)
            last_mm = [[None] * 2 for _ in range(2)]
            for kc in range(12):
                wsl = kc % 2
                evd = S_.dma("sync", f"wqb{wsl}", wqbF[wsl][:],
                             wqb_d[kc * 128:(kc + 1) * 128, G * 256:(G + 1) * 256],
                             [w_free2[wsl]])
                evw = S_.ve(lambda e, b=wqbF[wsl], kc=kc:
                            e.tensor_scalar(b[:], b[:], wlnq[:, kc:kc + 1], None, OP.mult),
                            [evd, ev_wlq])
                evh8, evl8 = hilo(wqbF[wsl][:], wqbh[wsl][:], wqbl[wsl][:], [evw],
                                  hfree=w_free2[wsl], lfree=w_free2[wsl])
                for j in range(2):
                    for i in range(2):
                        bidx = j * 2 + i
                        deps = ([ev_qr_last, evl8, bank_free4[bidx]] if kc == 0 else [evl8])
                        last_mm[j][i] = mm3(
                            pq[:, bidx * 512:(bidx + 1) * 512],
                            wqbh[wsl][:, j * 128:(j + 1) * 128],
                            wqbl[wsl][:, j * 128:(j + 1) * 128],
                            qrhA[:, kc * 1024 + i * 512: kc * 1024 + (i + 1) * 512],
                            qrlA[:, kc * 1024 + i * 512: kc * 1024 + (i + 1) * 512],
                            first=(kc == 0), last=(kc == 11), deps=deps)
                w_free2[wsl] = last_mm[1][1]
            for j in range(2):
                h = G * 2 + j
                for i in range(2):
                    ts = tsg * 2 + i
                    bidx = j * 2 + i
                    pap = pq[:, bidx * 512:(bidx + 1) * 512]
                    sl2 = i
                    ev32 = S_.sc(lambda e, i2=pap, o=qn32[sl2]: e.copy(o[:], i2),
                                 [last_mm[j][i], stg_free2[sl2]])
                    evsq = S_.sc(lambda e, i2=pap: e.activation(sq3[:], i2, AF.Square),
                                 [last_mm[j][i], sq3_free])
                    evn = S_.te(lambda e, o=pf1f[0:16, 0:512],
                                a=heye[:, h * NH:(h + 1) * NH]:
                                e.matmul(o, a, sq3[:], start=True, stop=True),
                                [evsq, pf1_free2])
                    sq3_free = evn
                    evqn = S_.ve(lambda e, o=qnorm2[:, ts * 512:(ts + 1) * 512],
                                 i2=pf1f[0:16, 0:512]: e.tensor_tensor(o, o, i2, OP.add),
                                 [evn, qnorm_init, evqn_prev])
                    pf1_free2 = evqn
                    evqn_prev = evqn
                    evh9, evl9 = hilo(qn32[sl2][:], qnh_s[sl2][:], qnl_s[sl2][:], [ev32],
                                      hfree=stg_free2[sl2], lfree=stg_free2[sl2])
                    S_.dma("scalar", f"spqn{sl2}", qnh_d[h][:, ts * 512:(ts + 1) * 512],
                           qnh_s[sl2][:], [evh9])
                    e2 = S_.dma("scalar", f"spqn{sl2}", qnl_d[h][:, ts * 512:(ts + 1) * 512],
                                qnl_s[sl2][:], [evl9])
                    stg_free2[sl2] = e2
                    bank_free4[bidx] = evsq
                    last_any = e2
            S_.barrier()
        qr_free = last_any

    S_.barrier()
    pqn.close()
    pqe = ExitStack()

    def sb1c(n, s, d):
        return pqe.enter_context(nc.sbuf_tensor(n, s, d))

    qpxh = sb1c("qpxh", [DR + 1, 8 * 128], BF16)
    qpxl = sb1c("qpxl", [DR + 1, 8 * 128], BF16)
    qsth = sb1c("qsth", [128, 12 * 512], BF16)
    qstl = sb1c("qstl", [128, 12 * 512], BF16)
    wpeF = [sb1c(f"wpeF{i}", [128, 512], F32) for i in range(2)]
    wpeh = [sb1c(f"wpeh{i}", [128, 512], BF16) for i in range(2)]
    wpel = [sb1c(f"wpel{i}", [128, 512], BF16) for i in range(2)]
    cq8 = [sb1c(f"cq8_{i}", [128, 512], F32) for i in range(2)]
    sq8 = [sb1c(f"sq8_{i}", [128, 512], F32) for i in range(2)]
    rp1 = sb1c("rp1", [128, 512], F32)
    rp2 = sb1c("rp2", [128, 512], F32)
    qpe32 = sb1c("qpe32", [128, 512], F32)
    qpeh = sb1c("qpeh", [128, 512], BF16)
    qpel = sb1c("qpel", [128, 512], BF16)
    qpn_all = sb1c("qpn_all", [128, 16 * 16], F32)
    dmp64 = sb1c("dmp64", [128, 64], F32)

    w_free3 = [None, None]
    qst_free = None
    rope_free = []
    cq_free = [None, None]
    qpx_free = None
    for ps_ in range(2):
        for tcg in range(4):
            ev_qst = None
            for kc in range(12):
                S_.dma("sync", "qst", qsth[:, kc * 512:(kc + 1) * 512],
                       qah_d[kc][:, tcg * 512:(tcg + 1) * 512], [qst_free])
                ev_qst = S_.dma("sync", "qst", qstl[:, kc * 512:(kc + 1) * 512],
                                qal_d[kc][:, tcg * 512:(tcg + 1) * 512], [qst_free])
            last_mm = [None] * 4
            for kc in range(12):
                wsl = kc % 2
                evd = S_.dma("sync", f"wpe{wsl}", wpeF[wsl][:],
                             wqb_d[kc * 128:(kc + 1) * 128,
                                   2048 + ps_ * 512: 2048 + (ps_ + 1) * 512],
                             [w_free3[wsl]])
                evw = S_.ve(lambda e, b=wpeF[wsl], kc=kc:
                            e.tensor_scalar(b[:], b[:], wlnq[:, kc:kc + 1], None, OP.mult),
                            [evd, ev_wlq])
                evh10, evl10 = hilo(wpeF[wsl][:], wpeh[wsl][:], wpel[wsl][:], [evw],
                                    hfree=w_free3[wsl], lfree=w_free3[wsl])
                for ti in range(4):
                    deps = (([ev_qst, evl10] + list(rope_free or [])) if kc == 0 else [evl10])
                    last_mm[ti] = mm3(pq[:, ti * 512:(ti + 1) * 512],
                                      qsth[:, kc * 512 + ti * 128: kc * 512 + (ti + 1) * 128],
                                      qstl[:, kc * 512 + ti * 128: kc * 512 + (ti + 1) * 128],
                                      wpeh[wsl][:], wpel[wsl][:],
                                      first=(kc == 0), last=(kc == 11), deps=deps)
                w_free3[wsl] = last_mm[3]
            qst_free = last_mm[3]
            for ti in range(4):
                tc = tcg * 4 + ti
                csl = ti % 2
                evc1 = S_.dma("sync", f"cqc{csl}", cq8[csl][:],
                              cosq8_d[tc * 128:(tc + 1) * 128, :], [cq_free[csl]])
                evc2 = S_.dma("sync", f"cqs{csl}", sq8[csl][:],
                              srotq8_d[tc * 128:(tc + 1) * 128, :], [cq_free[csl]])
                pap = pq[:, ti * 512:(ti + 1) * 512]
                pap3 = pap.rearrange("p (g d) -> p g d", d=64)
                rp23 = rp2[:].rearrange("p (g d) -> p g d", d=64)
                sq83 = sq8[csl][:].rearrange("p (g d) -> p g d", d=64)
                evr1 = S_.ve(lambda e, i2=pap, c=cq8[csl]:
                             e.tensor_tensor(rp1[:], i2, c[:], OP.mult),
                             [last_mm[ti], evc1] + list(rope_free or []))
                evr2 = S_.ve(lambda e, i3=pap3, s3=sq83:
                             e.tensor_tensor(rp23[:, :, 0:32], i3[:, :, 32:64],
                                             s3[:, :, 0:32], OP.mult), [evc2])
                evr3 = S_.ve(lambda e, i3=pap3, s3=sq83:
                             e.tensor_tensor(rp23[:, :, 32:64], i3[:, :, 0:32],
                                             s3[:, :, 32:64], OP.mult), [])
                evr4 = S_.ve(lambda e: e.tensor_tensor(qpe32[:], rp1[:], rp2[:], OP.add), [])
                cq_free[csl] = evr3
                evnq = None
                for gh in range(8):
                    hh = ps_ * 8 + gh
                    evnq = S_.sc(lambda e, g2=gh, a=qpn_all[:, tc * 16 + hh: tc * 16 + hh + 1]:
                                 e.activation(dmp64[:], qpe32[:, g2 * 64:(g2 + 1) * 64],
                                              AF.Square, accum_out=a), [evr4])
                evh11, evl11 = hilo(qpe32[:], qpeh[:], qpel[:], [evr4],
                                    hfree=ev_trans_last, lfree=ev_trans_last)
                rope_free = [evnq, evl11]
                evt1 = trans8([qpeh[:, g2 * 64:(g2 + 1) * 64] for g2 in range(8)],
                              64, pf0[:, :], idb[:], [evh11, qpx_free])
                evt2 = trans8([qpel[:, g2 * 64:(g2 + 1) * 64] for g2 in range(8)],
                              64, pf1[:, :], idb[:], [evl11, qpx_free])
                ev_trans_last = evt2
                eva = S_.sc(lambda e, o=qpxh[0:64, :], i2=pf0[0:64, :]:
                            e.copy(o, i2), [evt1])
                evb = S_.sc(lambda e, o=qpxl[0:64, :], i2=pf1[0:64, :]:
                            e.copy(o, i2), [evt2])
                if DBG and ps_ == 0 and tc == 4:
                    S_.dma("sync", "dbg", dbg_qpeh[:, :], qpeh[:], [eva, evb])
                    S_.dma("sync", "dbg", dbg_qpx[0:64, :], qpxh[0:64, :], [eva, evb])
                evsp = None
                for gh in range(8):
                    hh = ps_ * 8 + gh
                    S_.dma("scalar", "spqp", qph_d[hh][0:64, tc * 128:(tc + 1) * 128],
                           qpxh[0:64, gh * 128:(gh + 1) * 128], [eva])
                    evsp = S_.dma("scalar", "spqp", qpl_d[hh][0:64, tc * 128:(tc + 1) * 128],
                                  qpxl[0:64, gh * 128:(gh + 1) * 128], [evb])
                qpx_free = evsp

    for tc in range(16):
        evt = S_.te(lambda e, o=pf1f[0:16, 0:128], a=qpn_all[:, tc * 16:(tc + 1) * 16]:
                    e.matmul(o, a, idf[:], is_transpose=True, start=True, stop=True),
                    [pf1_free2, qpx_free, evqn_prev] if tc == 0 else [evqn_prev])
        evqn_prev = S_.ve(lambda e, o=qnorm2[:, tc * 128:(tc + 1) * 128],
                          i2=pf1f[0:16, 0:128]: e.tensor_tensor(o, o, i2, OP.add), [evt])

    if DBG:
        S_.dma("sync", "dbg", dbg_qn2[:, :], qnorm2[:], [evqn_prev])
        S_.dma("sync", "dbg", dbg_csc[:, :], csc[:], [evqn_prev])
    qn_s = sb1c("qn_s", [NH, OWN], F32)
    ch_s = sb1c("ch_s", [NH, OWN], BF16)
    cl_s = sb1c("cl_s", [NH, OWN], BF16)
    evc1 = S_.sc(lambda e: e.activation(qn_s[:], qnorm2[:], AF.Sqrt), [evqn_prev])
    evc2 = S_.ve(lambda e: e.tensor_scalar(qn_s[:], qn_s[:], csc[:], None, OP.mult), [evc1])
    evc3, evc4 = hilo(qn_s[:], ch_s[:], cl_s[:], [evc2])
    ev_crow = None
    for h in range(NH):
        S_.dma("sync", "spc", qph_d[h][64:65, :], ch_s[h:h + 1, :], [evc3, qpx_free])
        ev_crow = S_.dma("sync", "spc", qpl_d[h][64:65, :], cl_s[h:h + 1, :], [evc4])

    emit_sign_chunk(len(sgn_jobs))
    S_.barrier()
    pqe.close()
    pb_.close()

    p1 = ExitStack()

    def sb2(n, s, d):
        return p1.enter_context(nc.sbuf_tensor(n, s, d))

    mkb = sb2("mkb", [128, 8 * 512], BF16)
    for r in range(8):
        ev_mk = S_.dma("sync", "mk", mkb[:, r * 512:(r + 1) * 512], masks_d[r])
    knh_b = sb2("knh_b", [128, 2 * S], BF16)
    knl_b = sb2("knl_b", [128, 2 * S], BF16)
    vth_b = sb2("vth_b", [128, 2 * S], BF16)
    vtl_b = sb2("vtl_b", [128, 2 * S], BF16)
    qnh_b = sb2("qnh_b", [128, 2 * OWN], BF16)
    qnl_b = sb2("qnl_b", [128, 2 * OWN], BF16)
    qph_b = sb2("qph_b", [DR + 1, 2 * OWN], BF16)
    qpl_b = sb2("qpl_b", [DR + 1, 2 * OWN], BF16)
    tt32 = [sb2(f"tt32_{i}", [128, 512], F32) for i in range(2)]
    pex = [sb2(f"pex{i}", [128, 512], F32) for i in range(3)]
    phb = [sb2(f"phb{i}", [128, 512], BF16) for i in range(3)]
    plb = [sb2(f"plb{i}", [128, 512], BF16) for i in range(3)]
    recr = [sb2(f"recr{i}", [1, 512], F32) for i in range(2)]
    rech = [sb2(f"rech{i}", [1, 512], BF16) for i in range(2)]
    recl = [sb2(f"recl{i}", [1, 512], BF16) for i in range(2)]
    onerb = sb2("onerb", [1, 128], BF16)
    Rrep = [sb2(f"Rrep{i}", [128, 512], F32) for i in range(2)]
    o32 = [sb2(f"o32_{i}", [128, 512], F32) for i in range(2)]
    oh_s = [sb2(f"oh_s{i}", [128, 512], BF16) for i in range(2)]
    ol_s = [sb2(f"ol_s{i}", [128, 512], BF16) for i in range(2)]

    ev_onerb = S_.ve(lambda e: e.memset(onerb[:], 1.0), [])

    def load_head(h, dep):
        sl = h % 2
        S_.dma("sync", f"hd{sl}", knh_b[:, sl * S:(sl + 1) * S], knh_d[h], [dep])
        S_.dma("sync", f"hd{sl}", knl_b[:, sl * S:(sl + 1) * S], knl_d[h], [dep])
        S_.dma("sync", f"hd{sl}", vth_b[:, sl * S:(sl + 1) * S], vth_d[h], [dep])
        S_.dma("sync", f"hd{sl}", vtl_b[:, sl * S:(sl + 1) * S], vtl_d[h], [dep])
        S_.dma("sync", f"hd{sl}", qnh_b[:, sl * OWN:(sl + 1) * OWN], qnh_d[h], [dep])
        S_.dma("sync", f"hd{sl}", qnl_b[:, sl * OWN:(sl + 1) * OWN], qnl_d[h], [dep])
        S_.dma("sync", f"hd{sl}", qph_b[:, sl * OWN:(sl + 1) * OWN], qph_d[h], [dep])
        return S_.dma("sync", f"hd{sl}", qpl_b[:, sl * OWN:(sl + 1) * OWN], qpl_d[h], [dep])

    sc_banks = [pq[:, 0:512], pq[:, 512:1024], pq[:, 1024:1536]]
    av_banks = [pq[:, 1536:2048], pr[:, 0:512]]
    sum_banks = [pr[0:1, 512:1024], pf0f[0:1, 0:512]]
    sc_free = [ev_crow, ev_crow, ev_crow]
    av_free = [ev_crow, ev_crow]
    sum_free = [ev_crow, ev_crow]
    rec_free = [None, None]
    pf1_free3 = ev_crow
    o_free = [None, None]
    head_done = [None] * (NH + 2)
    ev_hload = [None] * NH
    ev_hload[0] = load_head(0, None)
    ev_hload[1] = load_head(1, None)

    for h in range(NH):
        sl = h % 2
        kno = sl * S
        qno = sl * OWN
        for J in range(4):
            KB = 8 * (J + 1)
            avb = av_banks[(h * 4 + J) % 2]
            smb = sum_banks[(h * 4 + J) % 2]
            ev_exp = {}
            ev_avlast = None
            ev_sumlast = None
            for kb in range(KB):
                sidx = kb % 3
                scb = sc_banks[sidx]
                deps0 = [ev_hload[h], ev_mk, sc_free[sidx]]
                S_.te(lambda e, o=scb, a=knh_b[:, kno + kb * 128: kno + (kb + 1) * 128],
                      b=qnh_b[:, qno + J * 512: qno + (J + 1) * 512]:
                      e.matmul(o, a, b, start=True, stop=False), deps0)
                S_.te(lambda e, o=scb, a=knh_b[:, kno + kb * 128: kno + (kb + 1) * 128],
                      b=qnl_b[:, qno + J * 512: qno + (J + 1) * 512]:
                      e.matmul(o, a, b, start=False, stop=False))
                S_.te(lambda e, o=scb, a=knl_b[:, kno + kb * 128: kno + (kb + 1) * 128],
                      b=qnh_b[:, qno + J * 512: qno + (J + 1) * 512]:
                      e.matmul(o, a, b, start=False, stop=False))
                S_.te(lambda e, o=scb, a=kpeh[:, kb * 128:(kb + 1) * 128],
                      b=qph_b[:, qno + J * 512: qno + (J + 1) * 512]:
                      e.matmul(o, a, b, start=False, stop=False))
                S_.te(lambda e, o=scb, a=kpeh[:, kb * 128:(kb + 1) * 128],
                      b=qpl_b[:, qno + J * 512: qno + (J + 1) * 512]:
                      e.matmul(o, a, b, start=False, stop=False))
                evmm = S_.te(lambda e, o=scb, a=kpel[:, kb * 128:(kb + 1) * 128],
                             b=qph_b[:, qno + J * 512: qno + (J + 1) * 512]:
                             e.matmul(o, a, b, start=False, stop=True))
                r = kb - 8 * J
                pix = kb % 3
                pfree = [ev_exp.get(kb - 3)]
                cons_free = ev_avlast if kb >= 3 else None
                if r >= 0:
                    evtt = S_.ve(lambda e, o=tt32[kb % 2], i2=scb, m=mkb[:, r * 512:(r + 1) * 512]:
                                 e.tensor_tensor(o[:], i2, m, OP.add),
                                 [evmm, ev_exp.get(kb - 2)])
                    evex = S_.sc(lambda e, o=pex[pix], i2=tt32[kb % 2]:
                                 e.activation(o[:], i2[:], AF.Exp, scale=SM),
                                 [evtt, cons_free])
                    sc_free[sidx] = evtt
                else:
                    evex = S_.sc(lambda e, o=pex[pix], i2=scb:
                                 e.activation(o[:], i2, AF.Exp, scale=SM),
                                 [evmm, cons_free])
                    sc_free[sidx] = evex
                ev_exp[kb] = evex
                evph = S_.ve(lambda e, o=phb[pix], i2=pex[pix]: e.tensor_copy(o[:], i2[:]),
                             [evex])
                evpl = S_.gp(lambda e, o=plb[pix], a=pex[pix], b=phb[pix]:
                             e.tensor_tensor(o[:], a[:], b[:], OP.subtract), [evph])
                dep_av = ([av_free[(h * 4 + J) % 2], sum_free[(h * 4 + J) % 2]]
                          if kb == 0 else [])
                S_.te(lambda e, o=avb, a=vth_b[:, kno + kb * 128: kno + (kb + 1) * 128],
                      b=phb[pix], s=(kb == 0):
                      e.matmul(o, a, b[:], start=s, stop=False), [evph, evpl] + dep_av)
                S_.te(lambda e, o=avb, a=vth_b[:, kno + kb * 128: kno + (kb + 1) * 128],
                      b=plb[pix]: e.matmul(o, a, b[:], start=False, stop=False))
                ev_avlast = S_.te(lambda e, o=avb,
                                  a=vtl_b[:, kno + kb * 128: kno + (kb + 1) * 128],
                                  b=phb[pix], p=(kb == KB - 1):
                                  e.matmul(o, a, b[:], start=False, stop=p))
                S_.te(lambda e, o=smb, a=onescb[:], b=phb[pix], s=(kb == 0):
                      e.matmul(o, a[:], b[:], start=s, stop=False))
                ev_sumlast = S_.te(lambda e, o=smb, a=onescb[:], b=plb[pix], p=(kb == KB - 1):
                                   e.matmul(o, a[:], b[:], start=False, stop=p))
            osl = (h * 4 + J) % 2
            evrc = S_.ve_strict(lambda e, i2=smb, o=recr[osl]: e.reciprocal(o[:], i2),
                                [ev_sumlast, rec_free[osl]])
            evrh = S_.ve(lambda e, o=rech[osl], i2=recr[osl]: e.tensor_copy(o[:], i2[:]),
                         [evrc, rec_free[osl]])
            evrl = S_.gp(lambda e, o=recl[osl], a=recr[osl], b=rech[osl]:
                         e.tensor_tensor(o[:], a[:], b[:], OP.subtract),
                         [evrh, rec_free[osl]])
            S_.te(lambda e, o=pf1f[:, 0:512], a=onerb, b=rech[osl]:
                  e.matmul(o, a[:], b[:], start=True, stop=False),
                  [evrh, evrl, ev_onerb, pf1_free3])
            evrp = S_.te(lambda e, o=pf1f[:, 0:512], a=onerb, b=recl[osl]:
                         e.matmul(o, a[:], b[:], start=False, stop=True))
            evR = S_.ve(lambda e, i2=pf1f[:, 0:512], o=Rrep[osl]: e.tensor_copy(o[:], i2),
                        [evrp])
            pf1_free3 = evR
            sum_free[osl] = evrc
            rec_free[osl] = evrp
            evo = S_.ve(lambda e, o=o32[osl], i2=avb, rr=Rrep[osl]:
                        e.tensor_tensor(o[:], i2, rr[:], OP.mult),
                        [evR, ev_avlast, o_free[osl]])
            av_free[osl] = evo
            evoh, evol = hilo(o32[osl][:], oh_s[osl][:], ol_s[osl][:], [evo],
                              hfree=o_free[osl], lfree=o_free[osl])
            S_.dma("scalar", f"spat{osl}", ath_d[h][:, J * 512:(J + 1) * 512], oh_s[osl][:],
                   [evoh])
            evsp = S_.dma("scalar", f"spat{osl}", atl_d[h][:, J * 512:(J + 1) * 512],
                          ol_s[osl][:], [evol])
            o_free[osl] = evsp
            if J == 3:
                head_done[h] = evsp
                if h + 2 < NH:
                    ev_hload[h + 2] = load_head(h + 2, evsp)

    S_.barrier()
    p1.close()
    kpe_scope.close()

    p2a = ExitStack()

    def sb3(n, s, d):
        return p2a.enter_context(nc.sbuf_tensor(n, s, d))

    atth = sb3("atth", [128, 16 * 512], BF16)
    attl = sb3("attl", [128, 16 * 512], BF16)
    xo = sb3("xo", [128, 4 * H], F32)
    r_tm = sb3("r_tm", [128, 4 * H], F32)
    woF = [sb3(f"woF{i}", [128, 384], F32) for i in range(2)]
    woh = [sb3(f"woh{i}", [128, 384], BF16) for i in range(2)]
    wol = [sb3(f"wol{i}", [128, 384], BF16) for i in range(2)]
    wof32 = sb3("wof32", [128, 512], F32)
    s1c = sb3("s1c", [128, 1], F32)
    s2c = sb3("s2c", [128, 1], F32)
    mc = sb3("mc", [128, 1], F32)
    vc = sb3("vc", [128, 1], F32)
    ac = sb3("ac", [128, 1], F32)
    zc = sb3("zc", [128, H], F32)
    scr8 = sb3("scr8", [128, H], F32)
    sqz = scr8
    sgc = scr8
    xhatc = sb3("xhatc", [128, H], F32)
    amaxc = sb3("amaxc", [128, 1], F32)
    sqc = sb3("sqc", [128, 1], F32)
    tqc = xhatc
    kqb = sb3("kqb", [128, H], BF16)
    ki32 = sb3("ki32", [128, H], mybir.dt.int32)
    kfc = sb3("kfc", [128, H], F32)
    gfc = sb3("gfc", [128, H], F32)
    gbc = sb3("gbc", [128, H], F32)
    xqst = sb3("xqst", [128, 1024], BF16)
    xq_d3 = xq_d.rearrange("p (c t) -> p c t", t=OWN)

    att_free = None
    xo_free = None
    w_free4 = [None, None]
    r_free = None
    wof_free = None
    tr_freeA = None
    z_spill = None
    xq_spill = None
    wo_bank_free = [None, None, None]
    for ts in range(4):
        ev_att = None
        for kc in range(16):
            S_.dma("sync", "att", atth[:, kc * 512:(kc + 1) * 512],
                   ath_d[kc][:, ts * 512:(ts + 1) * 512], [att_free])
            ev_att = S_.dma("sync", "att", attl[:, kc * 512:(kc + 1) * 512],
                            atl_d[kc][:, ts * 512:(ts + 1) * 512], [att_free])
        ev_xo = None
        for j in range(4):
            ev_xo = S_.dma("sync", "xo", xo[:, j * H:(j + 1) * H],
                           x_own[(ts * 4 + j) * 128:(ts * 4 + j + 1) * 128, :], [xo_free])
        ev_radd_all = None
        for og in range(6):
            ocs = list(range(og * 3, min(og * 3 + 3, 16)))
            last_mm = [None] * len(ocs)
            for kc in range(16):
                wsl = kc % 2
                w = 128 * len(ocs)
                evd = S_.dma("sync", f"wo{wsl}", woF[wsl][:, 0:w],
                             wo_d[kc * 128:(kc + 1) * 128,
                                  ocs[0] * 128: ocs[0] * 128 + w], [w_free4[wsl]])
                evh12, evl12 = hilo(woF[wsl][:, 0:w], woh[wsl][:, 0:w], wol[wsl][:, 0:w],
                                    [evd], hfree=w_free4[wsl], lfree=w_free4[wsl])
                for j in range(len(ocs)):
                    deps = ([ev_att, evl12, r_free, wo_bank_free[j]]
                            if kc == 0 else [evl12])
                    last_mm[j] = mm3(pq[:, j * 512:(j + 1) * 512],
                                     woh[wsl][:, j * 128:(j + 1) * 128],
                                     wol[wsl][:, j * 128:(j + 1) * 128],
                                     atth[:, kc * 512:(kc + 1) * 512],
                                     attl[:, kc * 512:(kc + 1) * 512],
                                     first=(kc == 0), last=(kc == 15), deps=deps)
                w_free4[wsl] = last_mm[-1]
            for j, oc in enumerate(ocs):
                pap = pq[:, j * 512:(j + 1) * 512]
                evv = S_.sc(lambda e, i2=pap: e.copy(wof32[:], i2),
                            [last_mm[j], wof_free])
                evt = trans8([wof32[:, u * 128:(u + 1) * 128] for u in range(4)],
                             128, pf0f[:, 0:512], idf[:], [evv, tr_freeA])
                wof_free = evt
                evradd = S_.ve(lambda e, oc=oc,
                               o=r_tm[:].rearrange("p (c f) -> p c f", f=H)[
                                   :, 0:4, oc * 128:(oc + 1) * 128],
                               i2=pf0f[:, 0:512].rearrange("p (c u) -> p c u", u=128),
                               x3=xo[:].rearrange("p (c f) -> p c f", f=H)[
                                   :, 0:4, oc * 128:(oc + 1) * 128]:
                               e.tensor_tensor(o, i2, x3, OP.add),
                               [evt, ev_xo, r_free])
                tr_freeA = evradd
                ev_radd_all = evradd
                wo_bank_free[j] = evv
        r_free = ev_radd_all
        for j in range(4):
            ch = ts * 4 + j
            rch = r_tm[:, j * H:(j + 1) * H]
            evs2 = S_.sc(lambda e, i2=rch: e.activation(sqz[:], i2, AF.Square,
                                                        accum_out=s2c[:]), [ev_radd_all])
            evs1 = S_.ve_strict(lambda e, i2=rch: e.reduce_sum(s1c[:], i2, axis=AX.X),
                         [ev_radd_all])
            evm1 = S_.ve(lambda e: e.tensor_scalar(mc[:], s1c[:], 1.0 / H, None, OP.mult),
                         [evs1])
            evm2 = S_.ve(lambda e: e.tensor_scalar(vc[:], s2c[:], 1.0 / H, None, OP.mult),
                         [evs2])
            evm3 = S_.ve(lambda e: e.tensor_tensor(ac[:], mc[:], mc[:], OP.mult), [evm1])
            evm4 = S_.ve(lambda e: e.tensor_tensor(vc[:], vc[:], ac[:], OP.subtract),
                         [evm2, evm3])
            evm5 = S_.ve(lambda e: e.tensor_scalar(vc[:], vc[:], EPS_LN, None, OP.add),
                         [evm4])
            evm6 = S_.sc(lambda e: e.activation(vc[:], vc[:], AF.Sqrt), [evm5])
            evm7 = S_.ve_strict(lambda e: e.reciprocal(ac[:], vc[:]), [evm6])
            evz = S_.ve(lambda e, i2=rch: e.tensor_scalar(zc[:], i2, mc[:], ac[:],
                                                          OP.subtract, OP.mult),
                        [evm7, z_spill])
            z_spill = S_.dma("scalar", "spz", z_d[ch], zc[:], [evz])
            evn2 = S_.sc(lambda e: e.activation(sqz[:], zc[:], AF.Square,
                                                accum_out=s2c[:]), [evz])
            evn1 = S_.ve_strict(lambda e: e.reduce_sum(s1c[:], zc[:], axis=AX.X), [evz])
            evn3 = S_.ve(lambda e: e.tensor_scalar(mc[:], s1c[:], 1.0 / H, None, OP.mult),
                         [evn1])
            evn4 = S_.ve(lambda e: e.tensor_scalar(vc[:], s2c[:], 1.0 / H, None, OP.mult),
                         [evn2])
            evn5 = S_.ve(lambda e: e.tensor_tensor(ac[:], mc[:], mc[:], OP.mult), [evn3])
            evn6 = S_.ve(lambda e: e.tensor_tensor(vc[:], vc[:], ac[:], OP.subtract),
                         [evn4, evn5])
            evn7 = S_.ve(lambda e: e.tensor_scalar(vc[:], vc[:], EPS_FFN, None, OP.add),
                         [evn6])
            evn8 = S_.sc(lambda e: e.activation(vc[:], vc[:], AF.Sqrt), [evn7])
            evn9 = S_.ve_strict(lambda e: e.reciprocal(ac[:], vc[:]), [evn8])
            evxh = S_.ve(lambda e: e.tensor_scalar(xhatc[:], zc[:], mc[:], ac[:],
                                                   OP.subtract, OP.mult), [evn9])
            evam = S_.ve_strict(lambda e: e.reduce_max(amaxc[:], xhatc[:], axis=AX.X,
                                                apply_absolute_value=True), [evxh])
            evra = S_.ve_strict(lambda e: e.reciprocal(sqc[:], amaxc[:]), [evam])
            evsc2 = S_.ve(lambda e: e.tensor_scalar(sqc[:], sqc[:], 127.0, None, OP.mult),
                          [evra])
            S_.ve(lambda e, ch=ch: e.tensor_scalar(rs_all[:, ch:ch + 1], amaxc[:],
                                                   1.0 / 127.0, None, OP.mult), [evam])
            evtq = S_.ve(lambda e: e.tensor_scalar(tqc[:], xhatc[:], sqc[:], None, OP.mult),
                         [evsc2])
            evki = S_.ve(lambda e: e.tensor_copy(ki32[:], tqc[:]), [evtq])
            evkf = S_.ve(lambda e: e.tensor_copy(kfc[:], ki32[:]), [evki])
            evf = S_.ve(lambda e: e.tensor_sub(gfc[:], tqc[:], kfc[:]), [evkf])
            evg1 = S_.ve(lambda e: e.tensor_scalar(gbc[:], gfc[:], 0.5000001, None,
                                                   OP.is_ge), [evf])
            evk1 = S_.ve(lambda e: e.tensor_add(kfc[:], kfc[:], gbc[:]), [evg1])
            evg2 = S_.ve(lambda e: e.tensor_scalar(gbc[:], gfc[:], -0.5000001, None,
                                                   OP.is_le), [evk1])
            evk2 = S_.ve(lambda e: e.tensor_sub(kfc[:], kfc[:], gbc[:]), [evg2])
            evt3 = S_.ve(lambda e: e.tensor_scalar(kfc[:], kfc[:], 127.0, -127.0,
                                                   OP.min, OP.max), [evk2])
            evkq = S_.ve(lambda e: e.tensor_copy(kqb[:], kfc[:]), [evt3])
            tiles = [kqb[:, u * 128:(u + 1) * 128] for u in range(16)]
            for b8 in range(2):
                evt4 = trans8(tiles[b8 * 8:(b8 + 1) * 8], 128, pf0[:, :], idb[:],
                              [evkq, xq_spill])
                evst = S_.sc(lambda e, o=xqst[:]: e.copy(o[:], pf0[:, :]), [evt4])
                xq_spill = S_.dma("scalar", "spxq",
                                  xq_d3[:, b8 * 8:(b8 + 1) * 8, ch * 128:(ch + 1) * 128],
                                  xqst[:].rearrange("p (c u) -> p c u", u=128), [evst])
        att_free = w_free4[1]
        xo_free = ev_radd_all

    S_.barrier()
    p2a.close()

    p2b = ExitStack()

    def sb4(n, s, d):
        return p2b.enter_context(nc.sbuf_tensor(n, s, d))

    xqg = sb4("xqg", [128, 16 * 256], BF16)
    zg = sb4("zg", [128, 2 * H], F32)
    sgw = [sb4(f"sgw{i}", [128, 16 * 512], BF16) for i in range(2)]
    sgdw = [sb4(f"sgdw{i}", [128, H], BF16) for i in range(2)]
    gg32 = sb4("gg32", [128, 512], F32)
    su32 = sb4("su32", [128, 512], F32)
    p32 = sb4("p32", [128, 512], F32)
    phf = sb4("phf", [128, 512], BF16)
    plf = sb4("plf", [128, 512], BF16)
    pfmh = sb4("pfmh", [128, 64 * 256], BF16)
    pfml = sb4("pfml", [128, 64 * 256], BF16)
    r2c = sb4("r2c", [128, H], F32)
    yc = sb4("yc", [128, H], F32)
    sqy = sb4("sqy", [128, H], F32)

    s1y = sb4("s1y", [128, 1], F32)
    s2y = sb4("s2y", [128, 1], F32)
    m_y = sb4("m_y", [128, 1], F32)
    v_y = sb4("v_y", [128, 1], F32)
    a_y = sb4("a_y", [128, 1], F32)
    pfmh3 = pfmh[:].rearrange("p (c u) -> p c u", u=256)
    pfml3 = pfml[:].rearrange("p (c u) -> p c u", u=256)

    xqg_free = None
    zg_free = None
    sgw_free = [None, None]
    sgd_free = [None, None]
    gu_free = [None, None, None, None]
    pf_free = None
    eps_free = None
    y_free = [None, None]
    dbank_free = None
    ev_y_last = None
    for g in range(8):
        ev_xqg = None
        for cc in range(16):
            ev_xqg = S_.dma("sync", "xqg", xqg[:, cc * 256:(cc + 1) * 256],
                            xq_d3[:, cc, g * 256:(g + 1) * 256], [xqg_free])
        ev_zg = None
        for i in range(2):
            ev_zg = S_.dma("sync", "zg", zg[:, i * H:(i + 1) * H], z_d[g * 2 + i],
                           [zg_free])
        ev_ptr_last = None
        for ft in range(16):
            wsl = ft % 2
            ev_sg = None
            for kc in range(16):
                ev_sg = S_.dma("sync", f"sgw{wsl}", sgw[wsl][:, kc * 512:(kc + 1) * 512],
                               sgg_d[kc * 128:(kc + 1) * 128, ft * 512:(ft + 1) * 512],
                               [sgw_free[wsl]])
            ev_gmm = [None, None]
            for tc in range(2):
                for kc in range(16):
                    deps = ([ev_xqg, ev_sg, gu_free[tc]] if kc == 0 else ())
                    ev_gmm[tc] = S_.te(
                        lambda e, o=pq[:, tc * 512:(tc + 1) * 512],
                        a=xqg[:, kc * 256 + tc * 128: kc * 256 + (tc + 1) * 128],
                        b=sgw[wsl][:, kc * 512:(kc + 1) * 512],
                        s=(kc == 0), p=(kc == 15):
                        e.matmul(o, a, b, start=s, stop=p), deps)
            ev_su = None
            for kc in range(16):
                ev_su = S_.dma("sync", f"sgw{wsl}", sgw[wsl][:, kc * 512:(kc + 1) * 512],
                               sgu_d[kc * 128:(kc + 1) * 128, ft * 512:(ft + 1) * 512],
                               [ev_gmm[1]])
            sgw_free[wsl] = None
            ev_umm = [None, None]
            for tc in range(2):
                for kc in range(16):
                    deps = ([ev_su, gu_free[2 + tc]] if kc == 0 else ())
                    ev_umm[tc] = S_.te(
                        lambda e, o=pq[:, (2 + tc) * 512:(3 + tc) * 512],
                        a=xqg[:, kc * 256 + tc * 128: kc * 256 + (tc + 1) * 128],
                        b=sgw[wsl][:, kc * 512:(kc + 1) * 512],
                        s=(kc == 0), p=(kc == 15):
                        e.matmul(o, a, b, start=s, stop=p), deps)
            sgw_free[wsl] = ev_umm[1]
            for tc in range(2):
                ch = g * 2 + tc
                evgv = S_.ve(lambda e, i2=pq[:, tc * 512:(tc + 1) * 512], ch=ch:
                             e.tensor_scalar(gg32[:], i2, rs_all[:, ch:ch + 1], None,
                                             OP.mult), [ev_gmm[tc], eps_free])
                evsu2 = S_.sc(lambda e: e.activation(su32[:], gg32[:], AF.Silu), [evgv])
                evp = S_.ve(lambda e, i2=pq[:, (2 + tc) * 512:(3 + tc) * 512], ch=ch:
                            e.scalar_tensor_tensor(p32[:], i2, rs_all[:, ch:ch + 1],
                                                   su32[:], OP.mult, OP.mult),
                            [ev_umm[tc], evsu2])
                gu_free[tc] = evp
                gu_free[2 + tc] = evp
                evph2, evpl2 = hilo(p32[:], phf[:], plf[:], [evp], hfree=eps_free,
                                    lfree=eps_free)
                evt5 = trans8([phf[:, u * 128:(u + 1) * 128] for u in range(4)] +
                              [plf[:, u * 128:(u + 1) * 128] for u in range(4)],
                              128, pf0[:, :], idb[:], [evph2, evpl2, pf_free])
                ev_ptr_last = evt5
                e1 = S_.sc(lambda e, o=pfmh3[:, ft * 4:(ft + 1) * 4, tc * 128:(tc + 1) * 128]:
                           e.copy(o, pf0h3), [evt5])
                eps_free = S_.sc(lambda e,
                                 o=pfml3[:, ft * 4:(ft + 1) * 4, tc * 128:(tc + 1) * 128]:
                                 e.copy(o, pf0l3), [evt5])
                pf_free = eps_free
        dbanks = [pq[:, 0:512], pq[:, 512:1024], pq[:, 1024:1536], pq[:, 1536:2048],
                  pr[:, 0:512], pr[:, 512:1024], pf0f[:, 0:512], pf1f[:, 0:512]]
        ev_dmm = None
        for fc in range(64):
            dsl = fc % 2
            ev_sd = S_.dma("sync", f"sgd{dsl}", sgdw[dsl][:],
                           sgd_d[fc * 128:(fc + 1) * 128, :], [sgd_free[dsl]])
            for pi, psrc in ((0, pfmh), (1, pfml)):
                for tc in range(2):
                    for os_ in range(4):
                        bidx = tc * 4 + os_
                        deps = ([ev_sd, eps_free, pf_free, dbank_free]
                                if (fc == 0 and pi == 0 and os_ == 0) else
                                ([ev_sd] if os_ == 0 and tc == 0 else ()))
                        ev_dmm = S_.te(
                            lambda e, o=dbanks[bidx],
                            a=psrc[:, fc * 256 + tc * 128: fc * 256 + (tc + 1) * 128],
                            b=sgdw[dsl][:, os_ * 512:(os_ + 1) * 512],
                            s=(fc == 0 and pi == 0), p=(fc == 63 and pi == 1):
                            e.matmul(o, a, b, start=s, stop=p), deps)
            sgd_free[dsl] = ev_dmm
        ev_c_last = None
        for tc in range(2):
            ch = g * 2 + tc
            ysl = tc
            if tc == 0:
                evr2 = S_.ve(lambda e, i2=pq[:, 0:2048], z3=zg[:, 0:H]:
                             e.tensor_tensor(r2c[:], i2, z3, OP.add),
                             [ev_dmm, ev_zg, y_free[ysl]])
            else:
                ev_a = S_.ve(lambda e, i2=pr[:, 0:1024], z3=zg[:, H:H + 1024]:
                             e.tensor_tensor(r2c[:, 0:1024], i2, z3, OP.add),
                             [ev_dmm, ev_zg, y_free[ysl]])
                ev_b = S_.ve(lambda e, i2=pf0f[:, 0:512], z3=zg[:, H + 1024:H + 1536]:
                             e.tensor_tensor(r2c[:, 1024:1536], i2, z3, OP.add), [ev_a])
                evr2 = S_.ve(lambda e, i2=pf1f[:, 0:512], z3=zg[:, H + 1536:2 * H]:
                             e.tensor_tensor(r2c[:, 1536:2048], i2, z3, OP.add), [ev_b])
            evs2b = S_.sc(lambda e: e.activation(sqy[:], r2c[:], AF.Square,
                                                 accum_out=s2y[:]), [evr2])
            evs1b = S_.ve_strict(lambda e: e.reduce_sum(s1y[:], r2c[:], axis=AX.X), [evr2])
            evb1 = S_.ve(lambda e: e.tensor_scalar(m_y[:], s1y[:], 1.0 / H, None, OP.mult),
                         [evs1b])
            evb2 = S_.ve(lambda e: e.tensor_scalar(v_y[:], s2y[:], 1.0 / H, None, OP.mult),
                         [evs2b])
            evb3 = S_.ve(lambda e: e.tensor_tensor(a_y[:], m_y[:], m_y[:], OP.mult), [evb1])
            evb4 = S_.ve(lambda e: e.tensor_tensor(v_y[:], v_y[:], a_y[:], OP.subtract),
                         [evb2, evb3])
            evb5 = S_.ve(lambda e: e.tensor_scalar(v_y[:], v_y[:], EPS_LN, None, OP.add),
                         [evb4])
            evb6 = S_.sc(lambda e: e.activation(v_y[:], v_y[:], AF.Sqrt), [evb5])
            evb7 = S_.ve_strict(lambda e: e.reciprocal(a_y[:], v_y[:]), [evb6])
            evy = S_.ve(lambda e: e.tensor_scalar(yc[:], r2c[:], m_y[:], a_y[:],
                                                  OP.subtract, OP.mult), [evb7])
            ev_y_last = S_.dma("scalar", "y", y_d[ch * 128:(ch + 1) * 128, :], yc[:],
                               [evy])
            y_free[ysl] = ev_y_last
            ev_c_last = evy
            if tc == 0:
                ev_c_tc0 = evr2
            else:
                ev_c_tc1 = evr2
        dbank_free = ev_c_last
        gu_free = [ev_c_tc0, ev_c_tc0, ev_c_tc0, ev_c_tc0]
        pf_free = ev_c_tc1
        eps_free = ev_c_tc1
        xqg_free = ev_ptr_last
        zg_free = ev_c_last


    S_.barrier()
    p2b.close()

    with nc.Block() as block:
        @block.sync
        def _(eng):
            for fn in S_.q["sync"]:
                fn(eng)

        @block.vector
        def _(eng):
            for fn in S_.q["vector"]:
                fn(eng)

        @block.scalar
        def _(eng):
            for fn in S_.q["scalar"]:
                fn(eng)

        @block.tensor
        def _(eng):
            for fn in S_.q["tensor"]:
                fn(eng)

        @block.gpsimd
        def _(eng):
            for fn in S_.q["gpsimd"]:
                fn(eng)

    es.close()
    return nc


def _rope_tables():
    inv = (1.0 / (10000.0 ** (np.arange(0, DR, 2, dtype=np.float64) / DR)))
    t = np.arange(S, dtype=np.float64)
    fr = np.outer(t, inv)
    emb = np.concatenate([fr, fr], axis=1)
    cos = np.cos(emb).astype(np.float32)
    sin = np.sin(emb).astype(np.float32)
    srot = sin.copy()
    srot[:, :DR // 2] *= -1.0
    return cos, srot


def kernel(L0_attn_hidden_states, L0_attn_Wqa, L0_attn_q_a_ln_w, L0_attn_Wqb,
           L0_attn_Wkva, L0_attn_kv_a_ln_w, L0_attn_Wkvb, L0_attn_Wo,
           L0_ffn_W_g, L0_ffn_W_u, L0_ffn_W_d):
    f32 = np.float32
    x = np.ascontiguousarray(L0_attn_hidden_states, f32)
    perm_q = np.concatenate([np.concatenate([np.arange(h * DQK, h * DQK + DN)
                                             for h in range(NH)]),
                             np.concatenate([np.arange(h * DQK + DN, (h + 1) * DQK)
                                             for h in range(NH)])])
    perm_kv = np.concatenate([np.concatenate([np.arange(h * (DN + DV), h * (DN + DV) + DN)
                                              for h in range(NH)]),
                              np.concatenate([np.arange(h * (DN + DV) + DN,
                                                        (h + 1) * (DN + DV))
                                              for h in range(NH)])])
    wqb_p = np.ascontiguousarray(np.asarray(L0_attn_Wqb, f32)[:, perm_q])
    wkvb_p = np.ascontiguousarray(np.asarray(L0_attn_Wkvb, f32)[:, perm_kv])
    wg_t = np.ascontiguousarray(np.asarray(L0_ffn_W_g, f32).T)
    wu_t = np.ascontiguousarray(np.asarray(L0_ffn_W_u, f32).T)
    wd_t = np.ascontiguousarray(np.asarray(L0_ffn_W_d, f32).T)
    cos, srot = _rope_tables()

    idf = np.eye(128, dtype=f32)
    idb = np.eye(128).astype(bfdt)
    onescb = np.ones((128, 1), bfdt)
    onesrf = np.ones((1, 128), f32)
    heye = np.zeros((NH, 128, NH), f32)
    for h in range(NH):
        heye[h, :, h] = 1.0

    common = dict(
        wqa=np.ascontiguousarray(L0_attn_Wqa, f32),
        wqb=wqb_p,
        wkva=np.ascontiguousarray(L0_attn_Wkva, f32),
        wkvb=wkvb_p,
        wo=np.ascontiguousarray(L0_attn_Wo, f32),
        wg_t=wg_t, wu_t=wu_t, wd_t=wd_t,
        wlnq=np.ascontiguousarray(L0_attn_q_a_ln_w, f32),
        wlnkv=np.ascontiguousarray(L0_attn_kv_a_ln_w, f32),
        cosk=cos, srotk=srot,
        idf=idf, idb=idb, onescb=onescb, onesrf=onesrf, heye=heye,
    )

    in_maps = []
    for c in range(8):
        b, p = c // 2, c % 2
        own_pos = np.arange(S).reshape(16, 256)[p::2].reshape(-1)
        m = dict(common)
        m["x_full"] = x[b]
        m["x_own"] = np.ascontiguousarray(x[b][own_pos])
        m["cosq8"] = np.ascontiguousarray(np.tile(cos[own_pos], (1, 8)))
        m["srotq8"] = np.ascontiguousarray(np.tile(srot[own_pos], (1, 8)))
        kk = np.arange(128)[None, :, None]
        ccol = np.arange(512)[None, None, :]
        rr = np.arange(8)[:, None, None]
        cc2 = ccol + 256 * (ccol >= 256)
        valid = (128 * rr + kk) <= (256 * p + cc2)
        m["masks"] = np.where(valid, 0.0, NEG).astype(bfdt)
        in_maps.append(m)

    nc = build_nc()
    res = run_bass_kernel_spmd(nc, in_maps, core_ids=list(range(8)))
    out = np.empty((B, S, H), f32)
    for c in range(8):
        b, p = c // 2, c % 2
        out[b].reshape(16, 256, H)[p::2] = res.results[c]["y"].reshape(8, 256, H)
    return out


# version 08 (final)
# speedup vs baseline: 1.4828x; 1.4828x over previous
import numpy as np
import ml_dtypes
from contextlib import ExitStack

import concourse.bass as bass
import concourse.mybir as mybir
from concourse.alu_op_type import AluOpType as OP
from concourse.bass_utils import run_bass_kernel_spmd

F32 = mybir.dt.float32
BF16 = mybir.dt.bfloat16
AF = mybir.ActivationFunctionType
AX = mybir.AxisListType
bfdt = ml_dtypes.bfloat16

B, S, H, NH = 4, 4096, 2048, 16
DN, DR, DV, QL, KVL = 128, 64, 128, 1536, 512
DQK = DN + DR
OWN = 2048
FF = 8192
SM = float(np.float32(DQK) ** np.float32(-0.5))
NEG = -1.0e9
EPS_RMS = 1e-6
EPS_LN = 1e-5
EPS_FFN = 1e-8

SCRATCH_KIND = "Internal"
DBG = False
PHASES = ("p0a", "p0b", "p1", "p2a", "p2b")


class Sched:
    ENG = ("sync", "vector", "scalar", "tensor", "gpsimd")
    COMPUTE_SEM = {"vector": "sV", "scalar": "sS", "tensor": "sT", "gpsimd": "sG"}

    def __init__(self, nc, es):
        self.nc = nc
        self._es = es
        self.q = {e: [] for e in self.ENG}
        self.sems = {}
        self.cnt = {}
        self.lastw = {}
        self.strict = set()
        self.last_ev = {}
        for s in ("dIn", "dSp", "dA", "dB", "sV", "sS", "sT", "sG"):
            self.sems[s] = es.enter_context(nc.semaphore(s))
            self.cnt[s] = 0

    def _wait(self, eng, dep):
        if dep is None:
            return
        s, v = dep
        if v <= 0:
            return
        key = (eng, s)
        if self.lastw.get(key, 0) >= v:
            return
        self.lastw[key] = v
        sem = self.sems[s]
        self.q[eng].append(lambda e, sem=sem, v=v: e.wait_ge(sem, v))

    CHAIN = ("vector", "scalar", "gpsimd")

    def add(self, eng, fn, deps=(), sem=None, amt=1, chain=True):
        if eng in self.CHAIN and chain:
            prev = self.last_ev.get(eng)
            if prev is not None:
                self._wait(eng, prev)
        for d in deps:
            self._wait(eng, d)
        if sem is None:
            self.q[eng].append(fn)
            return None
        h = self.sems[sem]
        self.cnt[sem] += amt
        v = self.cnt[sem]
        self.q[eng].append(lambda e, fn=fn, h=h, amt=amt: fn(e).then_inc(h, amt))
        if eng in self.CHAIN and chain:
            self.last_ev[eng] = (sem, v)
        return (sem, v)

    def stream_sem(self, name):
        if name not in self.sems:
            self.sems[name] = self._es.enter_context(self.nc.semaphore(name))
            self.cnt[name] = 0
        return name

    def dma(self, eng, qsem, out_ap, in_ap, deps=(), slow=False):
        qsem = self.stream_sem(qsem)
        if slow:
            return self.add(eng, lambda e, o=out_ap, i=in_ap:
                            e.dma_start(out=o, in_=i, allow_slow_non_contiguous=True),
                            deps, qsem, 16, chain=False)
        return self.add(eng, lambda e, o=out_ap, i=in_ap: e.dma_start(out=o, in_=i),
                        deps, qsem, 16, chain=False)

    def te(self, fn, deps=()):
        return self.add("tensor", fn, deps, "sT")

    def ve(self, fn, deps=()):
        return self.add("vector", fn, deps, "sV")

    def ve_strict(self, fn, deps=()):
        ev = self.add("vector", fn, deps, "sV")
        self.strict.add(ev)
        return ev

    def sc(self, fn, deps=()):
        return self.add("scalar", fn, deps, "sS")

    def gp(self, fn, deps=()):
        return self.add("gpsimd", fn, deps, "sG")

    def barrier(self):
        evs = [(s, self.cnt[s]) for s in self.cnt]
        for e in self.ENG:
            for ev in evs:
                self._wait(e, ev)


def build_nc():
    nc = bass.Bass("TRN2", target_bir_lowering=False, debug=False,
                   detect_race_conditions=False)
    es = ExitStack()
    S_ = Sched(nc, es)

    def dram(name, shape, dt, kind="ExternalInput"):
        return nc.dram_tensor(name, shape, dt, kind=kind).ap()

    x_full = dram("x_full", [S, H], F32)
    x_own = dram("x_own", [OWN, H], F32)
    wqa_d = dram("wqa", [H, QL], F32)
    wqb_d = dram("wqb", [QL, NH * DQK], F32)
    wkva_d = dram("wkva", [H, KVL + DR], F32)
    wkvb_d = dram("wkvb", [KVL, NH * (DN + DV)], F32)
    wo_d = dram("wo", [H, H], F32)
    wg_d = dram("wg_t", [H, FF], F32)
    wu_d = dram("wu_t", [H, FF], F32)
    wd_d = dram("wd_t", [FF, H], F32)
    wlnq_d = dram("wlnq", [QL], F32)
    wlnkv_d = dram("wlnkv", [KVL], F32)
    cosk_d = dram("cosk", [S, DR], F32)
    srotk_d = dram("srotk", [S, DR], F32)
    cosq8_d = dram("cosq8", [OWN, 512], F32)
    srotq8_d = dram("srotq8", [OWN, 512], F32)
    masks_d = dram("masks", [8, 128, 512], BF16)
    idf_d = dram("idf", [128, 128], F32)
    idb_d = dram("idb", [128, 128], BF16)
    onescb_d = dram("onescb", [128, 1], BF16)
    onesrf_d = dram("onesrf", [1, 128], F32)
    heye_d = dram("heye", [NH, 128, NH], F32)

    y_d = dram("y", [OWN, H], F32, kind="ExternalOutput")

    SK = SCRATCH_KIND
    sgg_d = dram("sg_g", [H, FF], BF16, kind=SK)
    sgu_d = dram("sg_u", [H, FF], BF16, kind=SK)
    sgd_d = dram("sg_d", [FF, H], BF16, kind=SK)
    knh_d = dram("kn_h", [NH, 128, S], BF16, kind=SK)
    knl_d = dram("kn_l", [NH, 128, S], BF16, kind=SK)
    vth_d = dram("vt_h", [NH, 128, S], BF16, kind=SK)
    vtl_d = dram("vt_l", [NH, 128, S], BF16, kind=SK)
    qah_d = dram("qa_h", [12, 128, OWN], BF16, kind=SK)
    qal_d = dram("qa_l", [12, 128, OWN], BF16, kind=SK)
    qnh_d = dram("qn_h", [NH, 128, OWN], BF16, kind=SK)
    qnl_d = dram("qn_l", [NH, 128, OWN], BF16, kind=SK)
    qph_d = dram("qp_h", [NH, DR + 1, OWN], BF16, kind=SK)
    qpl_d = dram("qp_l", [NH, DR + 1, OWN], BF16, kind=SK)
    ath_d = dram("at_h", [NH, 128, OWN], BF16, kind=SK)
    atl_d = dram("at_l", [NH, 128, OWN], BF16, kind=SK)
    z_d = dram("z_sp", [16, 128, H], F32, kind=SK)
    if DBG:
        dbg_ckvh = dram("dbg_ckvh", [128, 4 * S], BF16, kind="ExternalOutput")
        dbg_ckvl = dram("dbg_ckvl", [128, 4 * S], BF16, kind="ExternalOutput")
        dbg_qa32 = dram("dbg_qa32", [128, 4 * QL], F32, kind="ExternalOutput")
        dbg_nrm = dram("dbg_nrm", [128, 32], F32, kind="ExternalOutput")
        dbg_qpeh = dram("dbg_qpeh", [128, 512], BF16, kind="ExternalOutput")
        dbg_qn2 = dram("dbg_qn2", [NH, OWN], F32, kind="ExternalOutput")
        dbg_qr = dram("dbg_qr", [128, 12 * 1024], BF16, kind="ExternalOutput")
        dbg_qrl = dram("dbg_qrl", [128, 12 * 1024], BF16, kind="ExternalOutput")
        dbg_kn2 = dram("dbg_kn2", [NH, S], F32, kind="ExternalOutput")
        dbg_csc = dram("dbg_csc", [NH, 1], F32, kind="ExternalOutput")
        dbg_qpx = dram("dbg_qpx", [65, 1024], BF16, kind="ExternalOutput")
        dbg_rsq = dram("dbg_rsq", [128, 32], F32, kind="ExternalOutput")
        dbg_xq32 = dram("dbg_xq32", [128, 4 * H], F32, kind="ExternalOutput")
    xq_d = dram("xq_sp", [128, 16 * OWN], BF16, kind=SK)

    def sb(name, shape, dt):
        return es.enter_context(nc.sbuf_tensor(name, shape, dt))

    pq = es.enter_context(nc.psum_tensor("pq", [128, 2048], F32))
    pr = es.enter_context(nc.psum_tensor("pr", [128, 1024], F32))
    pf0 = es.enter_context(nc.psum_tensor("pf0", [128, 1024], BF16))
    pf1 = es.enter_context(nc.psum_tensor("pf1", [128, 1024], BF16))
    pf0f = pf0[:, :].bitcast(F32)
    pf1f = pf1[:, :].bitcast(F32)

    idf = sb("idf_s", [128, 128], F32)
    idb = sb("idb_s", [128, 128], BF16)
    onescb = sb("onescb_s", [128, 1], BF16)
    onesrf = sb("onesrf_s", [1, 128], F32)
    heye = sb("heye_s", [128, NH * NH], F32)
    csc = sb("csc", [NH, 1], F32)
    rs_all = sb("rs_all", [128, 16], F32)

    ev_c = None
    for t, d in ((idf, idf_d), (idb, idb_d), (onescb, onescb_d), (onesrf, onesrf_d)):
        ev_c = S_.dma("sync", "c0", t[:], d)
    for h in range(NH):
        ev_c = S_.dma("sync", "c0", heye[:, h * NH:(h + 1) * NH], heye_d[h])
    ev_consts = ev_c

    def hilo(f32_ap, h_ap, l_ap, deps, hfree=None, lfree=None):
        evh = S_.ve(lambda e, o=h_ap, i=f32_ap: e.tensor_copy(o, i),
                    list(deps) + [hfree])
        evl = S_.gp(lambda e, o=l_ap, a=f32_ap, b=h_ap: e.tensor_tensor(o, a, b, OP.subtract),
                    [evh, lfree] + list(deps))
        return evh, evl

    def mm3(psum_ap, lh, ll, rh, rl, first, last, deps=()):
        S_.te(lambda e, o=psum_ap, a=lh, b=rh, s=first:
              e.matmul(o, a, b, start=s, stop=False), deps)
        S_.te(lambda e, o=psum_ap, a=lh, b=rl:
              e.matmul(o, a, b, start=False, stop=False))
        return S_.te(lambda e, o=psum_ap, a=ll, b=rh, p=last:
                     e.matmul(o, a, b, start=False, stop=p))

    def trans8(srcs, bank_part_rows, bank, ident, deps):
        ev = None
        n = len(srcs)
        for i, a in enumerate(srcs):
            w = a.shape[1]
            o = bank[0:w, i * 128:(i + 1) * 128]
            ev = S_.te(lambda e, o=o, a=a, r=ident, s=(i == 0), p=(i == n - 1):
                       e.matmul(o, a, r, is_transpose=True, start=s, stop=p),
                       deps if i == 0 else ())
        return ev

    sgn_jobs = []
    for src, dst, K, ncols in ((wg_d, sgg_d, H, FF), (wu_d, sgu_d, H, FF),
                               (wd_d, sgd_d, FF, H)):
        for r0 in range(0, K, 128):
            for c0 in range(0, ncols, 512):
                sgn_jobs.append((src, dst, r0, c0))
    NSG = 6
    sg_scope = ExitStack()
    sgin = [sg_scope.enter_context(nc.sbuf_tensor(f"sgin{i}", [128, 512], F32))
            for i in range(NSG)]
    sgout = [sg_scope.enter_context(nc.sbuf_tensor(f"sgout{i}", [128, 512], BF16))
             for i in range(NSG)]
    sg_state = {"i": 0, "fin": [None] * NSG, "fout": [None] * NSG}
    kpe_scope = ExitStack()
    kpeh = kpe_scope.enter_context(nc.sbuf_tensor("kpeh", [DR + 1, S], BF16))
    kpel = kpe_scope.enter_context(nc.sbuf_tensor("kpel", [DR + 1, S], BF16))

    def emit_sign_chunk(k=1):
        for _ in range(k):
            i = sg_state["i"]
            if i >= len(sgn_jobs):
                return
            sg_state["i"] = i + 1
            src, dst, r0, c0 = sgn_jobs[i]
            sl = i % NSG
            evi = S_.dma("sync", "sgi", sgin[sl][:], src[r0:r0 + 128, c0:c0 + 512],
                         [sg_state["fin"][sl]])
            evs = S_.sc(lambda e, o=sgout[sl][:], x=sgin[sl][:]: e.activation(o, x, AF.Sign),
                        [evi, sg_state["fout"][sl]])
            evo = S_.dma("sync", "sgo", dst[r0:r0 + 128, c0:c0 + 512], sgout[sl][:], [evs])
            sg_state["fin"][sl] = evs
            sg_state["fout"][sl] = evo

    p0 = ExitStack()

    def sb0(n, s, d):
        return p0.enter_context(nc.sbuf_tensor(n, s, d))

    ckvh = sb0("ckvh", [128, 4 * S], BF16)
    ckvl = sb0("ckvl", [128, 4 * S], BF16)
    kpn = sb0("kpn", [128, 32], F32)
    knorm2 = sb0("knorm2", [NH, S], F32)
    kpnT = sb0("kpnT", [32, 128], F32)
    mx1 = sb0("mx1", [32, 1], F32)
    mx2 = sb0("mx2", [1, 32], F32)
    mkp2 = sb0("mkp2", [1, 1], F32)
    pkva = ExitStack()

    def sbk(n, s, d):
        return pkva.enter_context(nc.sbuf_tensor(n, s, d))

    xt = [sbk(f"xt{i}", [128, H], F32) for i in range(2)]
    xf32s = sbk("xf32s", [128, H], F32)
    xf32 = [xf32s, xf32s]
    xh = [sbk(f"xh{i}", [128, H], BF16) for i in range(2)]
    xl = [sbk(f"xl{i}", [128, H], BF16) for i in range(2)]
    wkvaF = sbk("wkvaF", [128, 576], F32)
    wkvah = sbk("wkvah", [128, 16 * 576], BF16)
    wkval = sbk("wkval", [128, 16 * 576], BF16)
    sqd = sbk("sqd", [128, 576], F32)
    nrm = sbk("nrm", [128, 1], F32)
    rsq = sbk("rsq", [128, 1], F32)
    chat32 = sbk("chat32", [128, 512], F32)
    chath = sbk("chath", [128, 512], BF16)
    chatl = sbk("chatl", [128, 512], BF16)
    wlnkv = sbk("wlnkv_s", [128, 4], F32)
    costl = [sbk(f"costl{i}", [128, DR], F32) for i in range(2)]
    srotl = [sbk(f"srotl{i}", [128, DR], F32) for i in range(2)]
    kp32 = sbk("kp32", [128, DR], F32)
    kp32b = sbk("kp32b", [128, DR], F32)
    kpstg = sbk("kpstg", [DR, 128], F32)

    ev_wln = S_.dma("sync", "wln0", wlnkv[:], wlnkv_d.rearrange("(c p) -> p c", p=128), slow=True)

    prevl = None
    for kc in range(16):
        evd = S_.dma("sync", "wkva", wkvaF[:], wkva_d[kc * 128:(kc + 1) * 128, :], [prevl])
        evh, evl = hilo(wkvaF[:], wkvah[:, kc * 576:(kc + 1) * 576],
                        wkval[:, kc * 576:(kc + 1) * 576], [evd])
        prevl = evl
    ev_wkva_last = prevl

    ck_sets = [(pq[:, 0:512], pq[:, 512:576]), (pq[:, 1024:1536], pq[:, 1536:1600])]
    ck_free = [None, None]
    ck_free2 = [None, None]
    tr_free = [None, None]
    xt_free = [None, None]
    xf32_free = None
    xh_free = [None, None]
    kpstg_free = None
    ev_ck_ready = None
    ev_rope_done = [None, None]
    chat_free = None
    ckvh3 = ckvh[:].rearrange("p (c t) -> p c t", t=S)
    ckvl3 = ckvl[:].rearrange("p (c t) -> p c t", t=S)
    pf0h3 = pf0[:, 0:512].rearrange("p (c t) -> p c t", t=128)
    pf0l3 = pf0[:, 512:1024].rearrange("p (c t) -> p c t", t=128)

    for tc in range(32):
        sl = tc % 2
        emit_sign_chunk(12)
        evx = S_.dma("sync", f"xt{sl}", xt[sl][:], x_full[tc * 128:(tc + 1) * 128, :],
                     [xt_free[sl]])
        evc = S_.dma("sync", f"co{sl}", costl[sl][:], cosk_d[tc * 128:(tc + 1) * 128, :],
                     [ev_rope_done[sl]])
        evsr = S_.dma("sync", f"sr{sl}", srotl[sl][:], srotk_d[tc * 128:(tc + 1) * 128, :],
                      [ev_rope_done[sl]])
        last_ev = None
        for g in range(4):
            bank = pr[:, (g % 2) * 512:(g % 2) * 512 + 512]
            evt = trans8([xt[sl][:, (g * 4 + j) * 128:(g * 4 + j + 1) * 128]
                          for j in range(4)], 128, bank, idf[:],
                         [evx, ev_consts, tr_free[g % 2]])
            last_ev = S_.sc(lambda e, o=xf32[sl][:, g * 512:(g + 1) * 512], i=bank:
                            e.copy(o, i), [evt, xf32_free])
            tr_free[g % 2] = last_ev
        xt_free[sl] = last_ev
        evh, evl = hilo(xf32[sl][:], xh[sl][:], xl[sl][:], [last_ev],
                        hfree=xh_free[sl], lfree=xh_free[sl])
        xf32_free = evl

        cs = tc % 2
        pA, pB = ck_sets[cs]
        ev_mmA = ev_mmB = None
        for os_, pap, w in ((0, pA, 512), (1, pB, 64)):
            last_mm = None
            for kc in range(16):
                deps = ([evl, ev_wkva_last, ck_free[cs], ck_free2[cs]] if kc == 0 else ())
                last_mm = mm3(pap,
                              xh[sl][:, kc * 128:(kc + 1) * 128],
                              xl[sl][:, kc * 128:(kc + 1) * 128],
                              wkvah[:, kc * 576 + os_ * 512: kc * 576 + os_ * 512 + w],
                              wkval[:, kc * 576 + os_ * 512: kc * 576 + os_ * 512 + w],
                              first=(kc == 0), last=(kc == 15), deps=deps)
            if os_ == 0:
                ev_mmA = last_mm
            else:
                ev_mmB = last_mm
        xh_free[sl] = ev_mmB

        evsq = S_.sc(lambda e, i=pA: e.activation(sqd[:, 0:512], i, AF.Square,
                                                  accum_out=nrm[:]), [ev_mmA])
        ev1 = S_.ve(lambda e: e.tensor_scalar(rsq[:], nrm[:], 1.0 / KVL, EPS_RMS,
                                              OP.mult, OP.add), [evsq])
        ev2 = S_.sc(lambda e: e.activation(rsq[:], rsq[:], AF.Sqrt), [ev1])
        ev3 = S_.ve_strict(lambda e: e.reciprocal(rsq[:], rsq[:]), [ev2])
        if DBG:
            S_.dma("sync", "dbg", dbg_nrm[:, tc:tc + 1], nrm[:], [ev1], slow=True)
            S_.dma("sync", "dbg", dbg_rsq[:, tc:tc + 1], rsq[:], [ev3], slow=True)
        evch = S_.ve(lambda e, i=pA: e.tensor_scalar(chat32[:], i, rsq[:], None, OP.mult),
                     [ev3, chat_free])
        evh2, evl2 = hilo(chat32[:], chath[:], chatl[:], [evch])
        evtr = trans8([chath[:, j * 128:(j + 1) * 128] for j in range(4)] +
                      [chatl[:, j * 128:(j + 1) * 128] for j in range(4)],
                      128, pf0[:, :], idb[:], [evh2, evl2, chat_free])
        evcka = S_.sc(lambda e, o=ckvh3[:, 0:4, tc * 128:(tc + 1) * 128], i=pf0h3:
                      e.copy(o, i), [evtr])
        evckb = S_.sc(lambda e, o=ckvl3[:, 0:4, tc * 128:(tc + 1) * 128], i=pf0l3:
                      e.copy(o, i), [evtr])
        ev_ck_ready = evckb
        chat_free = evckb

        evr1 = S_.ve(lambda e, i=pB, c=costl[sl]: e.tensor_tensor(kp32[:], i, c[:], OP.mult),
                     [ev_mmB, evc])
        evr2 = S_.ve(lambda e, i=pB, sr=srotl[sl]:
                     e.tensor_tensor(kp32b[:, 0:32], i[:, 32:64], sr[:, 0:32], OP.mult),
                     [evsr])
        evr3 = S_.ve(lambda e, i=pB, sr=srotl[sl]:
                     e.tensor_tensor(kp32b[:, 32:64], i[:, 0:32], sr[:, 32:64], OP.mult), [])
        evr4 = S_.ve(lambda e: e.tensor_tensor(kp32[:], kp32[:], kp32b[:], OP.add), [])
        evkn = S_.sc(lambda e, a=kpn[:, tc:tc + 1]:
                     e.activation(sqd[:, 0:64], kp32[:], AF.Square, accum_out=a), [evr4])
        evtp = S_.te(lambda e, o=pf1f[0:64, 0:128]:
                     e.matmul(o, kp32[:], idf[:], is_transpose=True, start=True, stop=True),
                     [evr4, evckb])
        evke = S_.sc(lambda e, i=pf1f[0:64, 0:128]: e.copy(kpstg[:], i),
                     [evtp, kpstg_free])
        evkh_, evkl_ = hilo(kpstg[:], kpeh[0:64, tc * 128:(tc + 1) * 128],
                            kpel[0:64, tc * 128:(tc + 1) * 128], [evke])
        kpstg_free = evkl_
        ev_rope_done[sl] = evke
        ck_free[cs] = evke
        ck_free2[cs] = evl2

    S_.ve(lambda e: e.memset(kpeh[64:65, :], -1.0), [])
    S_.ve(lambda e: e.memset(kpel[64:65, :], 0.0), [])

    evq1 = S_.te(lambda e, o=pf1f[0:32, 0:128]:
                 e.matmul(o, kpn[:], idf[:], is_transpose=True, start=True, stop=True),
                 [ev_rope_done[0], ev_rope_done[1]])
    evq2 = S_.ve(lambda e, i=pf1f[0:32, 0:128]: e.tensor_copy(kpnT[:], i), [evq1])
    evq3 = S_.ve_strict(lambda e: e.reduce_max(mx1[:], kpnT[:], axis=AX.X), [evq2])
    evq4 = S_.te(lambda e, o=pf1f[0:1, 0:32]:
                 e.matmul(o, mx1[:], idf[0:32, 0:32], is_transpose=True,
                          start=True, stop=True), [evq3])
    evq5 = S_.ve(lambda e, i=pf1f[0:1, 0:32]: e.tensor_copy(mx2[:], i), [evq4])
    evq6 = S_.ve_strict(lambda e: e.reduce_max(mkp2[:], mx2[:], axis=AX.X), [evq5])

    if DBG:
        S_.barrier()
        S_.dma("sync", "dbg", dbg_ckvh[:, :], ckvh[:])
        S_.dma("sync", "dbg", dbg_ckvl[:, :], ckvl[:])

    S_.barrier()
    pkva.close()
    wkvbF = sb0("wkvbF", [128, 2048], F32)
    wkvbh = sb0("wkvbh", [128, 4 * 4096], BF16)
    wkvbl = sb0("wkvbl", [128, 4 * 4096], BF16)
    wlnkv2 = sb0("wlnkv2", [128, 4], F32)
    ev_wln2 = S_.dma("sync", "wln1", wlnkv2[:], wlnkv_d.rearrange("(c p) -> p c", p=128), slow=True)
    prevl = None
    for kc in range(4):
        for hf in range(2):
            evd = S_.dma("sync", "wkvb", wkvbF[:],
                         wkvb_d[kc * 128:(kc + 1) * 128, hf * 2048:(hf + 1) * 2048],
                         [prevl])
            evsc = S_.ve(lambda e, kc=kc: e.tensor_scalar(wkvbF[:], wkvbF[:],
                                                          wlnkv2[:, kc:kc + 1], None,
                                                          OP.mult), [evd, ev_wln2])
            evh3, evl3 = hilo(wkvbF[:],
                              wkvbh[:, kc * 4096 + hf * 2048: kc * 4096 + (hf + 1) * 2048],
                              wkvbl[:, kc * 4096 + hf * 2048: kc * 4096 + (hf + 1) * 2048],
                              [evsc])
            prevl = evl3
    ev_wkvb_last = prevl

    kn32 = [sb0(f"kn32_{i}", [128, 512], F32) for i in range(2)]
    knh_s = [sb0(f"knh_s{i}", [128, 512], BF16) for i in range(2)]
    knl_s = [sb0(f"knl_s{i}", [128, 512], BF16) for i in range(2)]
    vth_st = [sb0(f"vth_st{i}", [128, 512], BF16) for i in range(2)]
    vtl_st = [sb0(f"vtl_st{i}", [128, 512], BF16) for i in range(2)]
    sq2 = sb0("sq2", [128, 512], F32)

    ckv_all = ev_ck_ready
    kv_banks = [pq[:, 0:512], pq[:, 512:1024], pq[:, 1024:1536],
                pq[:, 1536:2048], pr[:, 0:512], pr[:, 512:1024]]
    bank_free = [[ck_free[0]], [ck_free[0]], [ck_free[1]], [ck_free[1]],
                 [tr_free[0]], [tr_free[1]]]
    st_free = [None, None]
    sq2_free = None
    pf0_free = ckv_all
    pf1_free = evq6
    evkn2_prev = None
    knorm_init = S_.ve(lambda e: e.memset(knorm2[:], 0.0), [])

    for g0, g1 in ((0, 3), (3, 6), (6, 8)):
        nts = g1 - g0
        for oc in range(32):
            emit_sign_chunk(3)
            bsel = [(oc % 2) * 3 + i for i in range(nts)]
            last_mm = [None] * nts
            for i, ts in enumerate(range(g0, g1)):
                pap = kv_banks[bsel[i]]
                for kc in range(4):
                    deps = (([ev_wkvb_last, ckv_all] + list(bank_free[bsel[i]] or []))
                            if kc == 0 else ())
                    last_mm[i] = mm3(
                        pap,
                        wkvbh[:, kc * 4096 + oc * 128: kc * 4096 + (oc + 1) * 128],
                        wkvbl[:, kc * 4096 + oc * 128: kc * 4096 + (oc + 1) * 128],
                        ckvh3[:, kc, ts * 512:(ts + 1) * 512],
                        ckvl3[:, kc, ts * 512:(ts + 1) * 512],
                        first=(kc == 0), last=(kc == 3), deps=deps)
            for i, ts in enumerate(range(g0, g1)):
                pap = kv_banks[bsel[i]]
                sl2 = (oc * nts + i) % 2
                ev32 = S_.sc(lambda e, i2=pap, o=kn32[sl2]: e.copy(o[:], i2),
                             [last_mm[i], st_free[sl2]])
                if oc < 16:
                    h = oc
                    evsqr = S_.sc(lambda e, i2=pap: e.activation(sq2[:], i2, AF.Square),
                                  [last_mm[i], sq2_free])
                    evn = S_.te(lambda e, o=pf1f[0:16, 0:512],
                                a=heye[:, h * NH:(h + 1) * NH]:
                                e.matmul(o, a, sq2[:], start=True, stop=True),
                                [evsqr, pf1_free])
                    sq2_free = evn
                    evkn2 = S_.ve(lambda e, o=knorm2[:, ts * 512:(ts + 1) * 512],
                                  i2=pf1f[0:16, 0:512]: e.tensor_tensor(o, o, i2, OP.add),
                                  [evn, knorm_init, evkn2_prev])
                    pf1_free = evkn2
                    evkn2_prev = evkn2
                    evh4, evl4 = hilo(kn32[sl2][:], knh_s[sl2][:], knl_s[sl2][:], [ev32],
                                      hfree=st_free[sl2], lfree=st_free[sl2])
                    e1 = S_.dma("scalar", f"spkn{sl2}", knh_d[h][:, ts * 512:(ts + 1) * 512],
                                knh_s[sl2][:], [evh4])
                    e2 = S_.dma("scalar", f"spkn{sl2}", knl_d[h][:, ts * 512:(ts + 1) * 512],
                                knl_s[sl2][:], [evl4])
                    st_free[sl2] = e2
                    bank_free[bsel[i]] = [evl4, evn]
                else:
                    h = oc - 16
                    evh5, evl5 = hilo(kn32[sl2][:], knh_s[sl2][:], knl_s[sl2][:], [ev32],
                                      hfree=st_free[sl2], lfree=st_free[sl2])
                    bank_free[bsel[i]] = [evl5]
                    evtr2 = trans8([knh_s[sl2][:, j * 128:(j + 1) * 128] for j in range(4)] +
                                   [knl_s[sl2][:, j * 128:(j + 1) * 128] for j in range(4)],
                                   128, pf0[:, :], idb[:], [evh5, evl5, pf0_free])
                    evv1 = S_.sc(lambda e, o=vth_st[sl2][:], i2=pf0[:, 0:512]:
                                 e.copy(o, i2), [evtr2, st_free[sl2]])
                    evv2 = S_.sc(lambda e, o=vtl_st[sl2][:], i2=pf0[:, 512:1024]:
                                 e.copy(o, i2), [evtr2])
                    pf0_free = evv2
                    e3 = S_.dma("scalar", f"spv{sl2}", vth_d[h][:, ts * 512:(ts + 1) * 512],
                                vth_st[sl2][:], [evv1])
                    e4 = S_.dma("scalar", f"spv{sl2}", vtl_d[h][:, ts * 512:(ts + 1) * 512],
                                vtl_st[sl2][:], [evv2])
                    st_free[sl2] = e4

    if DBG:
        S_.dma("sync", "dbg", dbg_kn2[:, :], knorm2[:], [evkn2_prev])
    knmax = sb0("knmax", [NH, 1], F32)
    evm = S_.ve_strict(lambda e: e.reduce_max(knmax[:], knorm2[:], axis=AX.X), [evkn2_prev])
    evrep = S_.te(lambda e, o=pf1f[0:16, 0:1]:
                  e.matmul(o, onesrf[0:1, 0:16], mkp2[:], start=True, stop=True),
                  [evq6, ev_consts, pf1_free])
    evmk = S_.ve(lambda e, i=pf1f[0:16, 0:1]: e.tensor_tensor(csc[:], knmax[:], i, OP.add),
                 [evm, evrep])
    evmk2 = S_.sc(lambda e: e.activation(csc[:], csc[:], AF.Sqrt), [evmk])

    S_.barrier()
    p0.close()

    pb_ = ExitStack()

    def sb1(n, s, d):
        return pb_.enter_context(nc.sbuf_tensor(n, s, d))

    wlnq = sb1("wlnq_s", [128, 12], F32)
    qnorm2 = sb1("qnorm2", [NH, OWN], F32)
    ev_wlq = S_.dma("sync", "wln2", wlnq[:], wlnq_d.rearrange("(c p) -> p c", p=128), slow=True)

    pqa = ExitStack()

    def sb1a(n, s, d):
        return pqa.enter_context(nc.sbuf_tensor(n, s, d))

    xtq = [sb1a(f"xtq{i}", [128, H], F32) for i in range(2)]
    xq32 = sb1a("xq32", [128, 4 * H], F32)
    xqh = sb1a("xqh", [128, 4 * H], BF16)
    xql = sb1a("xql", [128, 4 * H], BF16)
    wqaF = [sb1a(f"wqaF{i}", [128, 512], F32) for i in range(2)]
    wqah = [sb1a(f"wqah{i}", [128, 512], BF16) for i in range(2)]
    wqal = [sb1a(f"wqal{i}", [128, 512], BF16) for i in range(2)]
    qa32 = sb1a("qa32", [128, 4 * QL], F32)
    nqp = sb1a("nqp", [128, 12], F32)
    sqd1 = sb1a("sqd1", [128, 512], F32)
    nrm1 = sb1a("nrm1", [128, 1], F32)
    rsq1 = sb1a("rsq1", [128, 1], F32)
    qh32 = sb1a("qh32", [128, QL], F32)
    qhh = sb1a("qhh", [128, QL], BF16)
    qhl = sb1a("qhl", [128, QL], BF16)
    qafh = sb1a("qafh", [128, 12 * 512], BF16)
    qafl = sb1a("qafl", [128, 12 * 512], BF16)
    qafh3 = qafh[:].rearrange("p (c u) -> p c u", u=512)
    qafl3 = qafl[:].rearrange("p (c u) -> p c u", u=512)

    xt_free2 = [None, None]
    tr_free2 = [None, None]
    w_free = [None, None]
    bank_free3 = [None] * 4
    qaf_free = None
    ev_trans_last = None
    for qg in range(4):
        emit_sign_chunk(16)
        ev_ready = [None] * 4
        for t in range(4):
            tc = qg * 4 + t
            sl = tc % 2
            evx = S_.dma("sync", f"xtq{sl}", xtq[sl][:], x_own[tc * 128:(tc + 1) * 128, :],
                         [xt_free2[sl]])
            last_ev = None
            for g in range(4):
                bank = pr[:, (g % 2) * 512:(g % 2) * 512 + 512]
                evt = trans8([xtq[sl][:, (g * 4 + j) * 128:(g * 4 + j + 1) * 128]
                              for j in range(4)], 128, bank, idf[:],
                             [evx, ev_consts, tr_free2[g % 2]])
                last_ev = S_.sc(lambda e, o=xq32[:, t * H + g * 512: t * H + (g + 1) * 512],
                                i=bank: e.copy(o, i), [evt])
                tr_free2[g % 2] = last_ev
            xt_free2[sl] = last_ev
            evh, evl = hilo(xq32[:, t * H:(t + 1) * H], xqh[:, t * H:(t + 1) * H],
                            xql[:, t * H:(t + 1) * H], [last_ev])
            ev_ready[t] = evl
        for os_ in range(3):
            last_mm = [None] * 4
            for kc in range(16):
                wsl = kc % 2
                evd = S_.dma("sync", f"wqa{wsl}", wqaF[wsl][:],
                             wqa_d[kc * 128:(kc + 1) * 128, os_ * 512:(os_ + 1) * 512],
                             [w_free[wsl]])
                evh6, evl6 = hilo(wqaF[wsl][:], wqah[wsl][:], wqal[wsl][:], [evd],
                                  hfree=w_free[wsl], lfree=w_free[wsl])
                for t in range(4):
                    deps = ([ev_ready[t], evl6, bank_free3[t]] if kc == 0 else [evl6])
                    last_mm[t] = mm3(pq[:, t * 512:(t + 1) * 512],
                                     xqh[:, t * H + kc * 128: t * H + (kc + 1) * 128],
                                     xql[:, t * H + kc * 128: t * H + (kc + 1) * 128],
                                     wqah[wsl][:], wqal[wsl][:],
                                     first=(kc == 0), last=(kc == 15), deps=deps)
                w_free[wsl] = last_mm[3]
            for t in range(4):
                pap = pq[:, t * 512:(t + 1) * 512]
                evv = S_.sc(lambda e, i2=pap,
                            o=qa32[:, t * QL + os_ * 512: t * QL + (os_ + 1) * 512]:
                            e.copy(o, i2), [last_mm[t]])
                evs = S_.sc(lambda e, i2=pap, a=nqp[:, t * 3 + os_: t * 3 + os_ + 1]:
                            e.activation(sqd1[:], i2, AF.Square, accum_out=a),
                            [last_mm[t]])
                bank_free3[t] = evs
        for t in range(4):
            tc = qg * 4 + t
            ev1 = S_.ve_strict(lambda e, t=t: e.reduce_sum(nrm1[:], nqp[:, t * 3:t * 3 + 3],
                                                    axis=AX.X), [bank_free3[t]])
            ev2 = S_.ve(lambda e: e.tensor_scalar(rsq1[:], nrm1[:], 1.0 / QL, EPS_RMS,
                                                  OP.mult, OP.add), [ev1])
            ev3 = S_.sc(lambda e: e.activation(rsq1[:], rsq1[:], AF.Sqrt), [ev2])
            ev4 = S_.ve_strict(lambda e: e.reciprocal(rsq1[:], rsq1[:]), [ev3])
            ev5 = S_.ve(lambda e, t=t: e.tensor_scalar(qh32[:], qa32[:, t * QL:(t + 1) * QL],
                                                       rsq1[:], None, OP.mult),
                        [ev4, ev_trans_last])
            evh7, evl7 = hilo(qh32[:], qhh[:], qhl[:], [ev5], hfree=ev_trans_last,
                              lfree=ev_trans_last)
            tiles = [qhh[:, j * 128:(j + 1) * 128] for j in range(12)] + \
                    [qhl[:, j * 128:(j + 1) * 128] for j in range(12)]
            prev_evac = qaf_free
            for bi in range(3):
                evt = trans8(tiles[bi * 8:(bi + 1) * 8], 128, pf0[:, :], idb[:],
                             [evh7, evl7, prev_evac])
                ev_trans_last = evt
                for half in range(2):
                    idx0 = bi * 8 + half * 4
                    is_h = idx0 < 12
                    cc = idx0 if is_h else idx0 - 12
                    dst3 = qafh3 if is_h else qafl3
                    o = dst3[:, cc:cc + 4, t * 128:(t + 1) * 128]
                    i3 = (pf0h3 if half == 0 else pf0l3)
                    prev_evac = S_.sc(lambda e, o=o, i3=i3: e.copy(o, i3), [evt])
            qaf_free = prev_evac
        if DBG and qg == 0:
            S_.barrier()
            S_.dma("sync", "dbg", dbg_qa32[:, :], qa32[:])
            S_.dma("sync", "dbg", dbg_xq32[:, :], xq32[:])
            S_.barrier()
        spill_ev = None
        for cc in range(12):
            S_.dma("scalar", "spqa", qah_d[cc][:, qg * 512:(qg + 1) * 512],
                   qafh[:, cc * 512:(cc + 1) * 512], [qaf_free])
            spill_ev = S_.dma("scalar", "spqa", qal_d[cc][:, qg * 512:(qg + 1) * 512],
                              qafl[:, cc * 512:(cc + 1) * 512], [qaf_free])
        qaf_free = spill_ev

    S_.barrier()
    pqa.close()

    pqn = ExitStack()

    def sb1b(n, s, d):
        return pqn.enter_context(nc.sbuf_tensor(n, s, d))

    qnorm_init = S_.ve(lambda e: e.memset(qnorm2[:], 0.0), [])
    wqbF = [sb1b(f"wqbF{i}", [128, 256], F32) for i in range(2)]
    wqbh = [sb1b(f"wqbh{i}", [128, 256], BF16) for i in range(2)]
    wqbl = [sb1b(f"wqbl{i}", [128, 256], BF16) for i in range(2)]
    qrhA = sb1b("qrhA", [128, 12 * 1024], BF16)
    qrlA = sb1b("qrlA", [128, 12 * 1024], BF16)
    qn32 = [sb1b(f"qn32_{i}", [128, 512], F32) for i in range(2)]
    qnh_s = [sb1b(f"qnh_s{i}", [128, 512], BF16) for i in range(2)]
    qnl_s = [sb1b(f"qnl_s{i}", [128, 512], BF16) for i in range(2)]
    sq3 = sb1b("sq3", [128, 512], F32)

    w_free2 = [None, None]
    stg_free2 = [None, None]
    sq3_free = None
    pf1_free2 = None
    evqn_prev = None
    qr_free = None
    for tsg in range(2):
        ev_qr_last = None
        for kc in range(12):
            S_.dma("sync", "qr", qrhA[:, kc * 1024:(kc + 1) * 1024],
                   qah_d[kc][:, tsg * 1024:(tsg + 1) * 1024], [qr_free])
            ev_qr_last = S_.dma("sync", "qr", qrlA[:, kc * 1024:(kc + 1) * 1024],
                                qal_d[kc][:, tsg * 1024:(tsg + 1) * 1024], [qr_free])
        if DBG and tsg == 0:
            S_.dma("sync", "dbg", dbg_qr[:, :], qrhA[:], [ev_qr_last])
            S_.dma("sync", "dbg", dbg_qrl[:, :], qrlA[:], [ev_qr_last])
        bank_free4 = [None] * 4
        last_any = None
        for G in range(8):
            emit_sign_chunk(2)
            last_mm = [[None] * 2 for _ in range(2)]
            for kc in range(12):
                wsl = kc % 2
                evd = S_.dma("sync", f"wqb{wsl}", wqbF[wsl][:],
                             wqb_d[kc * 128:(kc + 1) * 128, G * 256:(G + 1) * 256],
                             [w_free2[wsl]])
                evw = S_.ve(lambda e, b=wqbF[wsl], kc=kc:
                            e.tensor_scalar(b[:], b[:], wlnq[:, kc:kc + 1], None, OP.mult),
                            [evd, ev_wlq])
                evh8, evl8 = hilo(wqbF[wsl][:], wqbh[wsl][:], wqbl[wsl][:], [evw],
                                  hfree=w_free2[wsl], lfree=w_free2[wsl])
                for j in range(2):
                    for i in range(2):
                        bidx = j * 2 + i
                        deps = ([ev_qr_last, evl8, bank_free4[bidx]] if kc == 0 else [evl8])
                        last_mm[j][i] = mm3(
                            pq[:, bidx * 512:(bidx + 1) * 512],
                            wqbh[wsl][:, j * 128:(j + 1) * 128],
                            wqbl[wsl][:, j * 128:(j + 1) * 128],
                            qrhA[:, kc * 1024 + i * 512: kc * 1024 + (i + 1) * 512],
                            qrlA[:, kc * 1024 + i * 512: kc * 1024 + (i + 1) * 512],
                            first=(kc == 0), last=(kc == 11), deps=deps)
                w_free2[wsl] = last_mm[1][1]
            for j in range(2):
                h = G * 2 + j
                for i in range(2):
                    ts = tsg * 2 + i
                    bidx = j * 2 + i
                    pap = pq[:, bidx * 512:(bidx + 1) * 512]
                    sl2 = i
                    ev32 = S_.sc(lambda e, i2=pap, o=qn32[sl2]: e.copy(o[:], i2),
                                 [last_mm[j][i], stg_free2[sl2]])
                    evsq = S_.sc(lambda e, i2=pap: e.activation(sq3[:], i2, AF.Square),
                                 [last_mm[j][i], sq3_free])
                    evn = S_.te(lambda e, o=pf1f[0:16, 0:512],
                                a=heye[:, h * NH:(h + 1) * NH]:
                                e.matmul(o, a, sq3[:], start=True, stop=True),
                                [evsq, pf1_free2])
                    sq3_free = evn
                    evqn = S_.ve(lambda e, o=qnorm2[:, ts * 512:(ts + 1) * 512],
                                 i2=pf1f[0:16, 0:512]: e.tensor_tensor(o, o, i2, OP.add),
                                 [evn, qnorm_init, evqn_prev])
                    pf1_free2 = evqn
                    evqn_prev = evqn
                    evh9, evl9 = hilo(qn32[sl2][:], qnh_s[sl2][:], qnl_s[sl2][:], [ev32],
                                      hfree=stg_free2[sl2], lfree=stg_free2[sl2])
                    S_.dma("scalar", f"spqn{sl2}", qnh_d[h][:, ts * 512:(ts + 1) * 512],
                           qnh_s[sl2][:], [evh9])
                    e2 = S_.dma("scalar", f"spqn{sl2}", qnl_d[h][:, ts * 512:(ts + 1) * 512],
                                qnl_s[sl2][:], [evl9])
                    stg_free2[sl2] = e2
                    bank_free4[bidx] = evsq
                    last_any = e2
        qr_free = last_any

    S_.barrier()
    pqn.close()
    pqe = ExitStack()

    def sb1c(n, s, d):
        return pqe.enter_context(nc.sbuf_tensor(n, s, d))

    qpxh = sb1c("qpxh", [DR + 1, 8 * 128], BF16)
    qpxl = sb1c("qpxl", [DR + 1, 8 * 128], BF16)
    qsth = sb1c("qsth", [128, 12 * 512], BF16)
    qstl = sb1c("qstl", [128, 12 * 512], BF16)
    wpeF = [sb1c(f"wpeF{i}", [128, 512], F32) for i in range(2)]
    wpeh = [sb1c(f"wpeh{i}", [128, 512], BF16) for i in range(2)]
    wpel = [sb1c(f"wpel{i}", [128, 512], BF16) for i in range(2)]
    cq8 = [sb1c(f"cq8_{i}", [128, 512], F32) for i in range(2)]
    sq8 = [sb1c(f"sq8_{i}", [128, 512], F32) for i in range(2)]
    rp1 = sb1c("rp1", [128, 512], F32)
    rp2 = sb1c("rp2", [128, 512], F32)
    qpe32 = sb1c("qpe32", [128, 512], F32)
    qpeh = sb1c("qpeh", [128, 512], BF16)
    qpel = sb1c("qpel", [128, 512], BF16)
    qpn_all = sb1c("qpn_all", [128, 16 * 16], F32)
    dmp64 = sb1c("dmp64", [128, 64], F32)

    w_free3 = [None, None]
    qst_free = None
    rope_free = []
    cq_free = [None, None]
    qpx_free = None
    for ps_ in range(2):
        for tcg in range(4):
            ev_qst = None
            for kc in range(12):
                S_.dma("sync", "qst", qsth[:, kc * 512:(kc + 1) * 512],
                       qah_d[kc][:, tcg * 512:(tcg + 1) * 512], [qst_free])
                ev_qst = S_.dma("sync", "qst", qstl[:, kc * 512:(kc + 1) * 512],
                                qal_d[kc][:, tcg * 512:(tcg + 1) * 512], [qst_free])
            last_mm = [None] * 4
            for kc in range(12):
                wsl = kc % 2
                evd = S_.dma("sync", f"wpe{wsl}", wpeF[wsl][:],
                             wqb_d[kc * 128:(kc + 1) * 128,
                                   2048 + ps_ * 512: 2048 + (ps_ + 1) * 512],
                             [w_free3[wsl]])
                evw = S_.ve(lambda e, b=wpeF[wsl], kc=kc:
                            e.tensor_scalar(b[:], b[:], wlnq[:, kc:kc + 1], None, OP.mult),
                            [evd, ev_wlq])
                evh10, evl10 = hilo(wpeF[wsl][:], wpeh[wsl][:], wpel[wsl][:], [evw],
                                    hfree=w_free3[wsl], lfree=w_free3[wsl])
                for ti in range(4):
                    deps = (([ev_qst, evl10] + list(rope_free or [])) if kc == 0 else [evl10])
                    last_mm[ti] = mm3(pq[:, ti * 512:(ti + 1) * 512],
                                      qsth[:, kc * 512 + ti * 128: kc * 512 + (ti + 1) * 128],
                                      qstl[:, kc * 512 + ti * 128: kc * 512 + (ti + 1) * 128],
                                      wpeh[wsl][:], wpel[wsl][:],
                                      first=(kc == 0), last=(kc == 11), deps=deps)
                w_free3[wsl] = last_mm[3]
            qst_free = last_mm[3]
            for ti in range(4):
                tc = tcg * 4 + ti
                csl = ti % 2
                evc1 = S_.dma("sync", f"cqc{csl}", cq8[csl][:],
                              cosq8_d[tc * 128:(tc + 1) * 128, :], [cq_free[csl]])
                evc2 = S_.dma("sync", f"cqs{csl}", sq8[csl][:],
                              srotq8_d[tc * 128:(tc + 1) * 128, :], [cq_free[csl]])
                pap = pq[:, ti * 512:(ti + 1) * 512]
                pap3 = pap.rearrange("p (g d) -> p g d", d=64)
                rp23 = rp2[:].rearrange("p (g d) -> p g d", d=64)
                sq83 = sq8[csl][:].rearrange("p (g d) -> p g d", d=64)
                evr1 = S_.ve(lambda e, i2=pap, c=cq8[csl]:
                             e.tensor_tensor(rp1[:], i2, c[:], OP.mult),
                             [last_mm[ti], evc1] + list(rope_free or []))
                evr2 = S_.ve(lambda e, i3=pap3, s3=sq83:
                             e.tensor_tensor(rp23[:, :, 0:32], i3[:, :, 32:64],
                                             s3[:, :, 0:32], OP.mult), [evc2])
                evr3 = S_.ve(lambda e, i3=pap3, s3=sq83:
                             e.tensor_tensor(rp23[:, :, 32:64], i3[:, :, 0:32],
                                             s3[:, :, 32:64], OP.mult), [])
                evr4 = S_.ve(lambda e: e.tensor_tensor(qpe32[:], rp1[:], rp2[:], OP.add), [])
                cq_free[csl] = evr3
                evnq = None
                for gh in range(8):
                    hh = ps_ * 8 + gh
                    evnq = S_.sc(lambda e, g2=gh, a=qpn_all[:, tc * 16 + hh: tc * 16 + hh + 1]:
                                 e.activation(dmp64[:], qpe32[:, g2 * 64:(g2 + 1) * 64],
                                              AF.Square, accum_out=a), [evr4])
                evh11, evl11 = hilo(qpe32[:], qpeh[:], qpel[:], [evr4],
                                    hfree=ev_trans_last, lfree=ev_trans_last)
                rope_free = [evnq, evl11]
                evt1 = trans8([qpeh[:, g2 * 64:(g2 + 1) * 64] for g2 in range(8)],
                              64, pf0[:, :], idb[:], [evh11, qpx_free])
                evt2 = trans8([qpel[:, g2 * 64:(g2 + 1) * 64] for g2 in range(8)],
                              64, pf1[:, :], idb[:], [evl11, qpx_free])
                ev_trans_last = evt2
                eva = S_.sc(lambda e, o=qpxh[0:64, :], i2=pf0[0:64, :]:
                            e.copy(o, i2), [evt1])
                evb = S_.sc(lambda e, o=qpxl[0:64, :], i2=pf1[0:64, :]:
                            e.copy(o, i2), [evt2])
                if DBG and ps_ == 0 and tc == 4:
                    S_.dma("sync", "dbg", dbg_qpeh[:, :], qpeh[:], [eva, evb])
                    S_.dma("sync", "dbg", dbg_qpx[0:64, :], qpxh[0:64, :], [eva, evb])
                evsp = None
                for gh in range(8):
                    hh = ps_ * 8 + gh
                    S_.dma("scalar", "spqp", qph_d[hh][0:64, tc * 128:(tc + 1) * 128],
                           qpxh[0:64, gh * 128:(gh + 1) * 128], [eva])
                    evsp = S_.dma("scalar", "spqp", qpl_d[hh][0:64, tc * 128:(tc + 1) * 128],
                                  qpxl[0:64, gh * 128:(gh + 1) * 128], [evb])
                qpx_free = evsp

    for tc in range(16):
        evt = S_.te(lambda e, o=pf1f[0:16, 0:128], a=qpn_all[:, tc * 16:(tc + 1) * 16]:
                    e.matmul(o, a, idf[:], is_transpose=True, start=True, stop=True),
                    [pf1_free2, qpx_free, evqn_prev] if tc == 0 else [evqn_prev])
        evqn_prev = S_.ve(lambda e, o=qnorm2[:, tc * 128:(tc + 1) * 128],
                          i2=pf1f[0:16, 0:128]: e.tensor_tensor(o, o, i2, OP.add), [evt])

    if DBG:
        S_.dma("sync", "dbg", dbg_qn2[:, :], qnorm2[:], [evqn_prev])
        S_.dma("sync", "dbg", dbg_csc[:, :], csc[:], [evqn_prev])
    qn_s = sb1c("qn_s", [NH, OWN], F32)
    ch_s = sb1c("ch_s", [NH, OWN], BF16)
    cl_s = sb1c("cl_s", [NH, OWN], BF16)
    evc1 = S_.sc(lambda e: e.activation(qn_s[:], qnorm2[:], AF.Sqrt), [evqn_prev])
    evc2 = S_.ve(lambda e: e.tensor_scalar(qn_s[:], qn_s[:], csc[:], None, OP.mult), [evc1])
    evc3, evc4 = hilo(qn_s[:], ch_s[:], cl_s[:], [evc2])
    ev_crow = None
    for h in range(NH):
        S_.dma("sync", "spc", qph_d[h][64:65, :], ch_s[h:h + 1, :], [evc3, qpx_free])
        ev_crow = S_.dma("sync", "spc", qpl_d[h][64:65, :], cl_s[h:h + 1, :], [evc4])

    emit_sign_chunk(len(sgn_jobs))
    S_.barrier()
    pqe.close()
    pb_.close()

    p1 = ExitStack()

    def sb2(n, s, d):
        return p1.enter_context(nc.sbuf_tensor(n, s, d))

    mkb = sb2("mkb", [128, 8 * 512], BF16)
    for r in range(8):
        ev_mk = S_.dma("sync", "mk", mkb[:, r * 512:(r + 1) * 512], masks_d[r])
    knh_b = sb2("knh_b", [128, 2 * S], BF16)
    knl_b = sb2("knl_b", [128, 2 * S], BF16)
    vth_b = sb2("vth_b", [128, 2 * S], BF16)
    vtl_b = sb2("vtl_b", [128, 2 * S], BF16)
    qnh_b = sb2("qnh_b", [128, 2 * OWN], BF16)
    qnl_b = sb2("qnl_b", [128, 2 * OWN], BF16)
    qph_b = sb2("qph_b", [DR + 1, 2 * OWN], BF16)
    qpl_b = sb2("qpl_b", [DR + 1, 2 * OWN], BF16)
    NPB = 5
    tt32 = [sb2(f"tt32_{i}", [128, 512], F32) for i in range(2)]
    pex = [sb2(f"pex{i}", [128, 512], F32) for i in range(NPB)]
    phb = [sb2(f"phb{i}", [128, 512], BF16) for i in range(NPB)]
    plb = [sb2(f"plb{i}", [128, 512], BF16) for i in range(NPB)]
    recr = [sb2(f"recr{i}", [1, 512], F32) for i in range(2)]
    rech = [sb2(f"rech{i}", [1, 512], BF16) for i in range(2)]
    recl = [sb2(f"recl{i}", [1, 512], BF16) for i in range(2)]
    onerb = sb2("onerb", [1, 128], BF16)
    Rrep = [sb2(f"Rrep{i}", [128, 512], F32) for i in range(2)]
    o32 = [sb2(f"o32_{i}", [128, 512], F32) for i in range(2)]
    oh_s = [sb2(f"oh_s{i}", [128, 512], BF16) for i in range(2)]
    ol_s = [sb2(f"ol_s{i}", [128, 512], BF16) for i in range(2)]

    ev_onerb = S_.ve(lambda e: e.memset(onerb[:], 1.0), [])

    def load_head(h, dep):
        sl = h % 2
        S_.dma("sync", f"hd{sl}", knh_b[:, sl * S:(sl + 1) * S], knh_d[h], [dep])
        S_.dma("sync", f"hd{sl}", knl_b[:, sl * S:(sl + 1) * S], knl_d[h], [dep])
        S_.dma("sync", f"hd{sl}", vth_b[:, sl * S:(sl + 1) * S], vth_d[h], [dep])
        S_.dma("sync", f"hd{sl}", vtl_b[:, sl * S:(sl + 1) * S], vtl_d[h], [dep])
        S_.dma("sync", f"hd{sl}", qnh_b[:, sl * OWN:(sl + 1) * OWN], qnh_d[h], [dep])
        S_.dma("sync", f"hd{sl}", qnl_b[:, sl * OWN:(sl + 1) * OWN], qnl_d[h], [dep])
        S_.dma("sync", f"hd{sl}", qph_b[:, sl * OWN:(sl + 1) * OWN], qph_d[h], [dep])
        return S_.dma("sync", f"hd{sl}", qpl_b[:, sl * OWN:(sl + 1) * OWN], qpl_d[h], [dep])

    sc_banks = [pq[:, 0:512], pq[:, 512:1024], pq[:, 1024:1536]]
    av_banks = [pq[:, 1536:2048], pr[:, 0:512]]
    sum_banks = [pr[0:1, 512:1024], pf0f[0:1, 0:512]]
    sc_free = [ev_crow, ev_crow, ev_crow]
    av_free = [ev_crow, ev_crow]
    sum_free = [ev_crow, ev_crow]
    rec_free = [None, None]
    pf1_free3 = ev_crow
    o_free = [None, None]
    head_done = [None] * (NH + 2)
    ev_hload = [None] * NH
    ev_hload[0] = load_head(0, None)
    ev_hload[1] = load_head(1, None)

    for h in range(NH):
        sl = h % 2
        kno = sl * S
        qno = sl * OWN
        for J in range(4):
            KB = 8 * (J + 1)
            avb = av_banks[(h * 4 + J) % 2]
            smb = sum_banks[(h * 4 + J) % 2]
            ev_exp = {}
            ev_av = {}
            ev_ph = {}
            ev_pl = {}
            ev_avlast = None
            ev_sumlast = None

            def emit_av(kb):
                nonlocal ev_avlast, ev_sumlast
                pix = kb % NPB
                dep_av = ([av_free[(h * 4 + J) % 2], sum_free[(h * 4 + J) % 2]]
                          if kb == 0 else [])
                S_.te(lambda e, o=avb, a=vth_b[:, kno + kb * 128: kno + (kb + 1) * 128],
                      b=phb[pix], s=(kb == 0):
                      e.matmul(o, a, b[:], start=s, stop=False),
                      [ev_ph[kb], ev_pl[kb]] + dep_av)
                S_.te(lambda e, o=avb, a=vth_b[:, kno + kb * 128: kno + (kb + 1) * 128],
                      b=plb[pix]: e.matmul(o, a, b[:], start=False, stop=False))
                ev_avlast = S_.te(lambda e, o=avb,
                                  a=vtl_b[:, kno + kb * 128: kno + (kb + 1) * 128],
                                  b=phb[pix], p=(kb == KB - 1):
                                  e.matmul(o, a, b[:], start=False, stop=p))
                S_.te(lambda e, o=smb, a=onescb[:], b=phb[pix], s=(kb == 0):
                      e.matmul(o, a[:], b[:], start=s, stop=False))
                ev_sumlast = S_.te(lambda e, o=smb, a=onescb[:], b=plb[pix], p=(kb == KB - 1):
                                   e.matmul(o, a[:], b[:], start=False, stop=p))
                ev_av[kb] = ev_sumlast

            for kb in range(KB):
                sidx = kb % 3
                scb = sc_banks[sidx]
                deps0 = [ev_hload[h], ev_mk, sc_free[sidx]]
                S_.te(lambda e, o=scb, a=knh_b[:, kno + kb * 128: kno + (kb + 1) * 128],
                      b=qnh_b[:, qno + J * 512: qno + (J + 1) * 512]:
                      e.matmul(o, a, b, start=True, stop=False), deps0)
                S_.te(lambda e, o=scb, a=knh_b[:, kno + kb * 128: kno + (kb + 1) * 128],
                      b=qnl_b[:, qno + J * 512: qno + (J + 1) * 512]:
                      e.matmul(o, a, b, start=False, stop=False))
                S_.te(lambda e, o=scb, a=knl_b[:, kno + kb * 128: kno + (kb + 1) * 128],
                      b=qnh_b[:, qno + J * 512: qno + (J + 1) * 512]:
                      e.matmul(o, a, b, start=False, stop=False))
                S_.te(lambda e, o=scb, a=kpeh[:, kb * 128:(kb + 1) * 128],
                      b=qph_b[:, qno + J * 512: qno + (J + 1) * 512]:
                      e.matmul(o, a, b, start=False, stop=False))
                S_.te(lambda e, o=scb, a=kpeh[:, kb * 128:(kb + 1) * 128],
                      b=qpl_b[:, qno + J * 512: qno + (J + 1) * 512]:
                      e.matmul(o, a, b, start=False, stop=False))
                evmm = S_.te(lambda e, o=scb, a=kpel[:, kb * 128:(kb + 1) * 128],
                             b=qph_b[:, qno + J * 512: qno + (J + 1) * 512]:
                             e.matmul(o, a, b, start=False, stop=True))
                r = kb - 8 * J
                pix = kb % NPB
                cons_free = ev_av.get(kb - NPB)
                if r >= 0:
                    evtt = S_.ve(lambda e, o=tt32[kb % 2], i2=scb, m=mkb[:, r * 512:(r + 1) * 512]:
                                 e.tensor_tensor(o[:], i2, m, OP.add),
                                 [evmm, ev_exp.get(kb - 2)])
                    evex = S_.sc(lambda e, o=pex[pix], i2=tt32[kb % 2]:
                                 e.activation(o[:], i2[:], AF.Exp, scale=SM),
                                 [evtt, cons_free])
                    sc_free[sidx] = evtt
                else:
                    evex = S_.sc(lambda e, o=pex[pix], i2=scb:
                                 e.activation(o[:], i2, AF.Exp, scale=SM),
                                 [evmm, cons_free])
                    sc_free[sidx] = evex
                ev_exp[kb] = evex
                evph = S_.ve(lambda e, o=phb[pix], i2=pex[pix]: e.tensor_copy(o[:], i2[:]),
                             [evex, ev_av.get(kb - NPB)])
                evpl = S_.ve(lambda e, o=plb[pix], a=pex[pix], b=phb[pix]:
                             e.tensor_tensor(o[:], a[:], b[:], OP.subtract), [evph])
                ev_ph[kb] = evph
                ev_pl[kb] = evpl
                if kb >= 2:
                    emit_av(kb - 2)
            emit_av(KB - 2)
            emit_av(KB - 1)
            osl = (h * 4 + J) % 2
            evrc = S_.ve_strict(lambda e, i2=smb, o=recr[osl]: e.reciprocal(o[:], i2),
                                [ev_sumlast, rec_free[osl]])
            evrh = S_.ve(lambda e, o=rech[osl], i2=recr[osl]: e.tensor_copy(o[:], i2[:]),
                         [evrc, rec_free[osl]])
            evrl = S_.ve(lambda e, o=recl[osl], a=recr[osl], b=rech[osl]:
                         e.tensor_tensor(o[:], a[:], b[:], OP.subtract),
                         [evrh, rec_free[osl]])
            S_.te(lambda e, o=pf1f[:, 0:512], a=onerb, b=rech[osl]:
                  e.matmul(o, a[:], b[:], start=True, stop=False),
                  [evrh, evrl, ev_onerb, pf1_free3])
            evrp = S_.te(lambda e, o=pf1f[:, 0:512], a=onerb, b=recl[osl]:
                         e.matmul(o, a[:], b[:], start=False, stop=True))
            evR = S_.ve(lambda e, i2=pf1f[:, 0:512], o=Rrep[osl]: e.tensor_copy(o[:], i2),
                        [evrp])
            pf1_free3 = evR
            sum_free[osl] = evrc
            rec_free[osl] = evrp
            evo = S_.ve(lambda e, o=o32[osl], i2=avb, rr=Rrep[osl]:
                        e.tensor_tensor(o[:], i2, rr[:], OP.mult),
                        [evR, ev_avlast, o_free[osl]])
            av_free[osl] = evo
            evoh, evol = hilo(o32[osl][:], oh_s[osl][:], ol_s[osl][:], [evo],
                              hfree=o_free[osl], lfree=o_free[osl])
            S_.dma("scalar", f"spat{osl}", ath_d[h][:, J * 512:(J + 1) * 512], oh_s[osl][:],
                   [evoh])
            evsp = S_.dma("scalar", f"spat{osl}", atl_d[h][:, J * 512:(J + 1) * 512],
                          ol_s[osl][:], [evol])
            o_free[osl] = evsp
            if J == 3:
                head_done[h] = evsp
                if h + 2 < NH:
                    ev_hload[h + 2] = load_head(h + 2, evsp)

    S_.barrier()
    p1.close()
    kpe_scope.close()
    sg_scope.close()

    p2a = ExitStack()

    def sb3(n, s, d):
        return p2a.enter_context(nc.sbuf_tensor(n, s, d))

    atth = sb3("atth", [128, 16 * 512], BF16)
    attl = sb3("attl", [128, 16 * 512], BF16)
    xo = sb3("xo", [128, 4 * H], F32)
    r_tm = sb3("r_tm", [128, 4 * H], F32)
    woF = [sb3(f"woF{i}", [128, 384], F32) for i in range(2)]
    woh = [sb3(f"woh{i}", [128, 384], BF16) for i in range(2)]
    wol = [sb3(f"wol{i}", [128, 384], BF16) for i in range(2)]
    wof32 = sb3("wof32", [128, 512], F32)
    s1c = sb3("s1c", [128, 1], F32)
    s2c = sb3("s2c", [128, 1], F32)
    mc = sb3("mc", [128, 1], F32)
    vc = sb3("vc", [128, 1], F32)
    ac = sb3("ac", [128, 1], F32)
    zc = sb3("zc", [128, H], F32)
    scr8 = sb3("scr8", [128, H], F32)
    sqz = scr8
    sgc = scr8
    xhatc = sb3("xhatc", [128, H], F32)
    amaxc = sb3("amaxc", [128, 1], F32)
    sqc = sb3("sqc", [128, 1], F32)
    tqc = xhatc
    kqb = sb3("kqb", [128, H], BF16)
    ki32 = sb3("ki32", [128, H], mybir.dt.int32)
    kfc = sb3("kfc", [128, H], F32)
    gfc = sb3("gfc", [128, H], F32)
    gbc = sb3("gbc", [128, H], F32)
    xqst = sb3("xqst", [128, 1024], BF16)
    xq_d3 = xq_d.rearrange("p (c t) -> p c t", t=OWN)

    att_free = None
    xo_free = None
    w_free4 = [None, None]
    r_free = None
    wof_free = None
    tr_freeA = None
    z_spill = None
    xq_spill = None
    wo_bank_free = [None, None, None]
    for ts in range(4):
        ev_att = None
        for kc in range(16):
            S_.dma("sync", "att", atth[:, kc * 512:(kc + 1) * 512],
                   ath_d[kc][:, ts * 512:(ts + 1) * 512], [att_free])
            ev_att = S_.dma("sync", "att", attl[:, kc * 512:(kc + 1) * 512],
                            atl_d[kc][:, ts * 512:(ts + 1) * 512], [att_free])
        ev_xo = None
        for j in range(4):
            ev_xo = S_.dma("sync", "xo", xo[:, j * H:(j + 1) * H],
                           x_own[(ts * 4 + j) * 128:(ts * 4 + j + 1) * 128, :], [xo_free])
        ev_radd_all = None
        for og in range(6):
            ocs = list(range(og * 3, min(og * 3 + 3, 16)))
            last_mm = [None] * len(ocs)
            for kc in range(16):
                wsl = kc % 2
                w = 128 * len(ocs)
                evd = S_.dma("sync", f"wo{wsl}", woF[wsl][:, 0:w],
                             wo_d[kc * 128:(kc + 1) * 128,
                                  ocs[0] * 128: ocs[0] * 128 + w], [w_free4[wsl]])
                evh12, evl12 = hilo(woF[wsl][:, 0:w], woh[wsl][:, 0:w], wol[wsl][:, 0:w],
                                    [evd], hfree=w_free4[wsl], lfree=w_free4[wsl])
                for j in range(len(ocs)):
                    deps = ([ev_att, evl12, r_free, wo_bank_free[j]]
                            if kc == 0 else [evl12])
                    last_mm[j] = mm3(pq[:, j * 512:(j + 1) * 512],
                                     woh[wsl][:, j * 128:(j + 1) * 128],
                                     wol[wsl][:, j * 128:(j + 1) * 128],
                                     atth[:, kc * 512:(kc + 1) * 512],
                                     attl[:, kc * 512:(kc + 1) * 512],
                                     first=(kc == 0), last=(kc == 15), deps=deps)
                w_free4[wsl] = last_mm[-1]
            for j, oc in enumerate(ocs):
                pap = pq[:, j * 512:(j + 1) * 512]
                evv = S_.sc(lambda e, i2=pap: e.copy(wof32[:], i2),
                            [last_mm[j], wof_free])
                evt = trans8([wof32[:, u * 128:(u + 1) * 128] for u in range(4)],
                             128, pf0f[:, 0:512], idf[:], [evv, tr_freeA])
                wof_free = evt
                evradd = S_.ve(lambda e, oc=oc,
                               o=r_tm[:].rearrange("p (c f) -> p c f", f=H)[
                                   :, 0:4, oc * 128:(oc + 1) * 128],
                               i2=pf0f[:, 0:512].rearrange("p (c u) -> p c u", u=128),
                               x3=xo[:].rearrange("p (c f) -> p c f", f=H)[
                                   :, 0:4, oc * 128:(oc + 1) * 128]:
                               e.tensor_tensor(o, i2, x3, OP.add),
                               [evt, ev_xo, r_free])
                tr_freeA = evradd
                ev_radd_all = evradd
                wo_bank_free[j] = evv
        r_free = ev_radd_all
        for j in range(4):
            ch = ts * 4 + j
            rch = r_tm[:, j * H:(j + 1) * H]
            evs2 = S_.sc(lambda e, i2=rch: e.activation(sqz[:], i2, AF.Square,
                                                        accum_out=s2c[:]), [ev_radd_all])
            evs1 = S_.ve_strict(lambda e, i2=rch: e.reduce_sum(s1c[:], i2, axis=AX.X),
                         [ev_radd_all])
            evm1 = S_.ve(lambda e: e.tensor_scalar(mc[:], s1c[:], 1.0 / H, None, OP.mult),
                         [evs1])
            evm2 = S_.ve(lambda e: e.tensor_scalar(vc[:], s2c[:], 1.0 / H, None, OP.mult),
                         [evs2])
            evm3 = S_.ve(lambda e: e.tensor_tensor(ac[:], mc[:], mc[:], OP.mult), [evm1])
            evm4 = S_.ve(lambda e: e.tensor_tensor(vc[:], vc[:], ac[:], OP.subtract),
                         [evm2, evm3])
            evm5 = S_.ve(lambda e: e.tensor_scalar(vc[:], vc[:], EPS_LN, None, OP.add),
                         [evm4])
            evm6 = S_.sc(lambda e: e.activation(vc[:], vc[:], AF.Sqrt), [evm5])
            evm7 = S_.ve_strict(lambda e: e.reciprocal(ac[:], vc[:]), [evm6])
            evz = S_.ve(lambda e, i2=rch: e.tensor_scalar(zc[:], i2, mc[:], ac[:],
                                                          OP.subtract, OP.mult),
                        [evm7, z_spill])
            z_spill = S_.dma("scalar", "spz", z_d[ch], zc[:], [evz])
            evn2 = S_.sc(lambda e: e.activation(sqz[:], zc[:], AF.Square,
                                                accum_out=s2c[:]), [evz])
            evn1 = S_.ve_strict(lambda e: e.reduce_sum(s1c[:], zc[:], axis=AX.X), [evz])
            evn3 = S_.ve(lambda e: e.tensor_scalar(mc[:], s1c[:], 1.0 / H, None, OP.mult),
                         [evn1])
            evn4 = S_.ve(lambda e: e.tensor_scalar(vc[:], s2c[:], 1.0 / H, None, OP.mult),
                         [evn2])
            evn5 = S_.ve(lambda e: e.tensor_tensor(ac[:], mc[:], mc[:], OP.mult), [evn3])
            evn6 = S_.ve(lambda e: e.tensor_tensor(vc[:], vc[:], ac[:], OP.subtract),
                         [evn4, evn5])
            evn7 = S_.ve(lambda e: e.tensor_scalar(vc[:], vc[:], EPS_FFN, None, OP.add),
                         [evn6])
            evn8 = S_.sc(lambda e: e.activation(vc[:], vc[:], AF.Sqrt), [evn7])
            evn9 = S_.ve_strict(lambda e: e.reciprocal(ac[:], vc[:]), [evn8])
            evxh = S_.ve(lambda e: e.tensor_scalar(xhatc[:], zc[:], mc[:], ac[:],
                                                   OP.subtract, OP.mult), [evn9])
            evam = S_.ve_strict(lambda e: e.reduce_max(amaxc[:], xhatc[:], axis=AX.X,
                                                apply_absolute_value=True), [evxh])
            evra = S_.ve_strict(lambda e: e.reciprocal(sqc[:], amaxc[:]), [evam])
            evsc2 = S_.ve(lambda e: e.tensor_scalar(sqc[:], sqc[:], 127.0, None, OP.mult),
                          [evra])
            S_.ve(lambda e, ch=ch: e.tensor_scalar(rs_all[:, ch:ch + 1], amaxc[:],
                                                   1.0 / 127.0, None, OP.mult), [evam])
            evtq = S_.ve(lambda e: e.tensor_scalar(tqc[:], xhatc[:], sqc[:], None, OP.mult),
                         [evsc2])
            evki = S_.ve(lambda e: e.tensor_copy(ki32[:], tqc[:]), [evtq])
            evkf = S_.ve(lambda e: e.tensor_copy(kfc[:], ki32[:]), [evki])
            evf = S_.ve(lambda e: e.tensor_sub(gfc[:], tqc[:], kfc[:]), [evkf])
            evg1 = S_.ve(lambda e: e.tensor_scalar(gbc[:], gfc[:], 0.5000001, None,
                                                   OP.is_ge), [evf])
            evk1 = S_.ve(lambda e: e.tensor_add(kfc[:], kfc[:], gbc[:]), [evg1])
            evg2 = S_.ve(lambda e: e.tensor_scalar(gbc[:], gfc[:], -0.5000001, None,
                                                   OP.is_le), [evk1])
            evk2 = S_.ve(lambda e: e.tensor_sub(kfc[:], kfc[:], gbc[:]), [evg2])
            evt3 = S_.ve(lambda e: e.tensor_scalar(kfc[:], kfc[:], 127.0, -127.0,
                                                   OP.min, OP.max), [evk2])
            evkq = S_.ve(lambda e: e.tensor_copy(kqb[:], kfc[:]), [evt3])
            tiles = [kqb[:, u * 128:(u + 1) * 128] for u in range(16)]
            for b8 in range(2):
                evt4 = trans8(tiles[b8 * 8:(b8 + 1) * 8], 128, pf0[:, :], idb[:],
                              [evkq, xq_spill])
                evst = S_.sc(lambda e, o=xqst[:]: e.copy(o[:], pf0[:, :]), [evt4])
                xq_spill = S_.dma("scalar", "spxq",
                                  xq_d3[:, b8 * 8:(b8 + 1) * 8, ch * 128:(ch + 1) * 128],
                                  xqst[:].rearrange("p (c u) -> p c u", u=128), [evst])
        att_free = w_free4[1]
        xo_free = ev_radd_all

    S_.barrier()
    p2a.close()

    p2b = ExitStack()

    def sb4(n, s, d):
        return p2b.enter_context(nc.sbuf_tensor(n, s, d))

    xqg = sb4("xqg", [128, 16 * 256], BF16)
    zg = sb4("zg", [128, 2 * H], F32)
    sgw = [sb4(f"sgw{i}", [128, 16 * 512], BF16) for i in range(3)]
    sgdw = [sb4(f"sgdw{i}", [128, H], BF16) for i in range(4)]
    gg32 = sb4("gg32", [128, 512], F32)
    su32 = sb4("su32", [128, 512], F32)
    p32 = sb4("p32", [128, 512], F32)
    phf = sb4("phf", [128, 512], BF16)
    plf = sb4("plf", [128, 512], BF16)
    pfmh = sb4("pfmh", [128, 64 * 256], BF16)
    pfml = sb4("pfml", [128, 64 * 256], BF16)
    r2c = sb4("r2c", [128, H], F32)
    yc = sb4("yc", [128, H], F32)
    sqy = sb4("sqy", [128, H], F32)

    s1y = sb4("s1y", [128, 1], F32)
    s2y = sb4("s2y", [128, 1], F32)
    m_y = sb4("m_y", [128, 1], F32)
    v_y = sb4("v_y", [128, 1], F32)
    a_y = sb4("a_y", [128, 1], F32)
    pfmh3 = pfmh[:].rearrange("p (c u) -> p c u", u=256)
    pfml3 = pfml[:].rearrange("p (c u) -> p c u", u=256)

    xqg_free = None
    zg_free = None
    sgw_cons = [None, None, None]
    sgw_item = {"k": 0}
    sgd_free = [None, None, None, None]

    def load_sgw(src, ft):
        k = sgw_item["k"]
        sgw_item["k"] = k + 1
        sl = k % 3
        qn = "sync" if k % 2 == 0 else "gpsimd"
        ev = None
        for kc in range(16):
            ev = S_.dma(qn, f"sgw{qn[0]}", sgw[sl][:, kc * 512:(kc + 1) * 512],
                        src[kc * 128:(kc + 1) * 128, ft * 512:(ft + 1) * 512],
                        [sgw_cons[sl]] if kc == 0 else ())
        return sl, ev
    gu_free = [None, None, None, None]
    pf_free = None
    eps_free = None
    y_free = [None, None]
    dbank_free = None
    ev_y_last = None
    for g in range(8):
        ev_xqg = None
        for cc in range(16):
            ev_xqg = S_.dma("sync", "xqg", xqg[:, cc * 256:(cc + 1) * 256],
                            xq_d3[:, cc, g * 256:(g + 1) * 256], [xqg_free])
        ev_zg = None
        for i in range(2):
            ev_zg = S_.dma("gpsimd", "zg", zg[:, i * H:(i + 1) * H], z_d[g * 2 + i],
                           [zg_free])
        ev_ptr_last = None
        for ft in range(16):
            gsl, ev_sg = load_sgw(sgg_d, ft)
            ev_gmm = [None, None]
            for tc in range(2):
                for kc in range(16):
                    deps = ([ev_xqg, ev_sg, gu_free[tc]] if kc == 0 else ())
                    ev_gmm[tc] = S_.te(
                        lambda e, o=pq[:, tc * 512:(tc + 1) * 512],
                        a=xqg[:, kc * 256 + tc * 128: kc * 256 + (tc + 1) * 128],
                        b=sgw[gsl][:, kc * 512:(kc + 1) * 512],
                        s=(kc == 0), p=(kc == 15):
                        e.matmul(o, a, b, start=s, stop=p), deps)
            sgw_cons[gsl] = ev_gmm[1]
            usl, ev_su = load_sgw(sgu_d, ft)
            ev_umm = [None, None]
            for tc in range(2):
                for kc in range(16):
                    deps = ([ev_su, gu_free[2 + tc]] if kc == 0 else ())
                    ev_umm[tc] = S_.te(
                        lambda e, o=pq[:, (2 + tc) * 512:(3 + tc) * 512],
                        a=xqg[:, kc * 256 + tc * 128: kc * 256 + (tc + 1) * 128],
                        b=sgw[usl][:, kc * 512:(kc + 1) * 512],
                        s=(kc == 0), p=(kc == 15):
                        e.matmul(o, a, b, start=s, stop=p), deps)
            sgw_cons[usl] = ev_umm[1]
            for tc in range(2):
                ch = g * 2 + tc
                evgv = S_.ve(lambda e, i2=pq[:, tc * 512:(tc + 1) * 512], ch=ch:
                             e.tensor_scalar(gg32[:], i2, rs_all[:, ch:ch + 1], None,
                                             OP.mult), [ev_gmm[tc], eps_free])
                evsu2 = S_.sc(lambda e: e.activation(su32[:], gg32[:], AF.Silu), [evgv])
                evp = S_.ve(lambda e, i2=pq[:, (2 + tc) * 512:(3 + tc) * 512], ch=ch:
                            e.scalar_tensor_tensor(p32[:], i2, rs_all[:, ch:ch + 1],
                                                   su32[:], OP.mult, OP.mult),
                            [ev_umm[tc], evsu2])
                gu_free[tc] = evp
                gu_free[2 + tc] = evp
                evph2, evpl2 = hilo(p32[:], phf[:], plf[:], [evp], hfree=eps_free,
                                    lfree=eps_free)
                evt5 = trans8([phf[:, u * 128:(u + 1) * 128] for u in range(4)] +
                              [plf[:, u * 128:(u + 1) * 128] for u in range(4)],
                              128, pf0[:, :], idb[:], [evph2, evpl2, pf_free])
                ev_ptr_last = evt5
                e1 = S_.sc(lambda e, o=pfmh3[:, ft * 4:(ft + 1) * 4, tc * 128:(tc + 1) * 128]:
                           e.copy(o, pf0h3), [evt5])
                eps_free = S_.sc(lambda e,
                                 o=pfml3[:, ft * 4:(ft + 1) * 4, tc * 128:(tc + 1) * 128]:
                                 e.copy(o, pf0l3), [evt5])
                pf_free = eps_free
        dbanks = [pq[:, 0:512], pq[:, 512:1024], pq[:, 1024:1536], pq[:, 1536:2048],
                  pr[:, 0:512], pr[:, 512:1024], pf0f[:, 0:512], pf1f[:, 0:512]]
        ev_dmm = None
        for fc in range(64):
            dsl = fc % 4
            ev_sd = S_.dma("sync" if fc % 2 == 0 else "gpsimd",
                           "sgdS" if fc % 2 == 0 else "sgdG", sgdw[dsl][:],
                           sgd_d[fc * 128:(fc + 1) * 128, :], [sgd_free[dsl]])
            for pi, psrc in ((0, pfmh), (1, pfml)):
                for tc in range(2):
                    for os_ in range(4):
                        bidx = tc * 4 + os_
                        deps = ([ev_sd, eps_free, pf_free, dbank_free]
                                if (fc == 0 and pi == 0 and os_ == 0) else
                                ([ev_sd] if os_ == 0 and tc == 0 else ()))
                        ev_dmm = S_.te(
                            lambda e, o=dbanks[bidx],
                            a=psrc[:, fc * 256 + tc * 128: fc * 256 + (tc + 1) * 128],
                            b=sgdw[dsl][:, os_ * 512:(os_ + 1) * 512],
                            s=(fc == 0 and pi == 0), p=(fc == 63 and pi == 1):
                            e.matmul(o, a, b, start=s, stop=p), deps)
            sgd_free[dsl] = ev_dmm
        ev_c_last = None
        for tc in range(2):
            ch = g * 2 + tc
            ysl = tc
            if tc == 0:
                evr2 = S_.ve(lambda e, i2=pq[:, 0:2048], z3=zg[:, 0:H]:
                             e.tensor_tensor(r2c[:], i2, z3, OP.add),
                             [ev_dmm, ev_zg, y_free[ysl]])
            else:
                ev_a = S_.ve(lambda e, i2=pr[:, 0:1024], z3=zg[:, H:H + 1024]:
                             e.tensor_tensor(r2c[:, 0:1024], i2, z3, OP.add),
                             [ev_dmm, ev_zg, y_free[ysl]])
                ev_b = S_.ve(lambda e, i2=pf0f[:, 0:512], z3=zg[:, H + 1024:H + 1536]:
                             e.tensor_tensor(r2c[:, 1024:1536], i2, z3, OP.add), [ev_a])
                evr2 = S_.ve(lambda e, i2=pf1f[:, 0:512], z3=zg[:, H + 1536:2 * H]:
                             e.tensor_tensor(r2c[:, 1536:2048], i2, z3, OP.add), [ev_b])
            evs2b = S_.sc(lambda e: e.activation(sqy[:], r2c[:], AF.Square,
                                                 accum_out=s2y[:]), [evr2])
            evs1b = S_.ve_strict(lambda e: e.reduce_sum(s1y[:], r2c[:], axis=AX.X), [evr2])
            evb1 = S_.ve(lambda e: e.tensor_scalar(m_y[:], s1y[:], 1.0 / H, None, OP.mult),
                         [evs1b])
            evb2 = S_.ve(lambda e: e.tensor_scalar(v_y[:], s2y[:], 1.0 / H, None, OP.mult),
                         [evs2b])
            evb3 = S_.ve(lambda e: e.tensor_tensor(a_y[:], m_y[:], m_y[:], OP.mult), [evb1])
            evb4 = S_.ve(lambda e: e.tensor_tensor(v_y[:], v_y[:], a_y[:], OP.subtract),
                         [evb2, evb3])
            evb5 = S_.ve(lambda e: e.tensor_scalar(v_y[:], v_y[:], EPS_LN, None, OP.add),
                         [evb4])
            evb6 = S_.sc(lambda e: e.activation(v_y[:], v_y[:], AF.Sqrt), [evb5])
            evb7 = S_.ve_strict(lambda e: e.reciprocal(a_y[:], v_y[:]), [evb6])
            evy = S_.ve(lambda e: e.tensor_scalar(yc[:], r2c[:], m_y[:], a_y[:],
                                                  OP.subtract, OP.mult), [evb7])
            ev_y_last = S_.dma("scalar", "y", y_d[ch * 128:(ch + 1) * 128, :], yc[:],
                               [evy])
            y_free[ysl] = ev_y_last
            ev_c_last = evy
            if tc == 0:
                ev_c_tc0 = evr2
            else:
                ev_c_tc1 = evr2
        dbank_free = ev_c_last
        gu_free = [ev_c_tc0, ev_c_tc0, ev_c_tc0, ev_c_tc0]
        pf_free = ev_c_tc1
        eps_free = ev_c_tc1
        xqg_free = ev_ptr_last
        zg_free = ev_c_last


    S_.barrier()
    p2b.close()

    with nc.Block() as block:
        @block.sync
        def _(eng):
            for fn in S_.q["sync"]:
                fn(eng)

        @block.vector
        def _(eng):
            for fn in S_.q["vector"]:
                fn(eng)

        @block.scalar
        def _(eng):
            for fn in S_.q["scalar"]:
                fn(eng)

        @block.tensor
        def _(eng):
            for fn in S_.q["tensor"]:
                fn(eng)

        @block.gpsimd
        def _(eng):
            for fn in S_.q["gpsimd"]:
                fn(eng)

    es.close()
    return nc


def _rope_tables():
    inv = (1.0 / (10000.0 ** (np.arange(0, DR, 2, dtype=np.float64) / DR)))
    t = np.arange(S, dtype=np.float64)
    fr = np.outer(t, inv)
    emb = np.concatenate([fr, fr], axis=1)
    cos = np.cos(emb).astype(np.float32)
    sin = np.sin(emb).astype(np.float32)
    srot = sin.copy()
    srot[:, :DR // 2] *= -1.0
    return cos, srot


def kernel(L0_attn_hidden_states, L0_attn_Wqa, L0_attn_q_a_ln_w, L0_attn_Wqb,
           L0_attn_Wkva, L0_attn_kv_a_ln_w, L0_attn_Wkvb, L0_attn_Wo,
           L0_ffn_W_g, L0_ffn_W_u, L0_ffn_W_d):
    f32 = np.float32
    x = np.ascontiguousarray(L0_attn_hidden_states, f32)
    perm_q = np.concatenate([np.concatenate([np.arange(h * DQK, h * DQK + DN)
                                             for h in range(NH)]),
                             np.concatenate([np.arange(h * DQK + DN, (h + 1) * DQK)
                                             for h in range(NH)])])
    perm_kv = np.concatenate([np.concatenate([np.arange(h * (DN + DV), h * (DN + DV) + DN)
                                              for h in range(NH)]),
                              np.concatenate([np.arange(h * (DN + DV) + DN,
                                                        (h + 1) * (DN + DV))
                                              for h in range(NH)])])
    wqb_p = np.ascontiguousarray(np.asarray(L0_attn_Wqb, f32)[:, perm_q])
    wkvb_p = np.ascontiguousarray(np.asarray(L0_attn_Wkvb, f32)[:, perm_kv])
    wg_t = np.ascontiguousarray(np.asarray(L0_ffn_W_g, f32).T)
    wu_t = np.ascontiguousarray(np.asarray(L0_ffn_W_u, f32).T)
    wd_t = np.ascontiguousarray(np.asarray(L0_ffn_W_d, f32).T)
    cos, srot = _rope_tables()

    idf = np.eye(128, dtype=f32)
    idb = np.eye(128).astype(bfdt)
    onescb = np.ones((128, 1), bfdt)
    onesrf = np.ones((1, 128), f32)
    heye = np.zeros((NH, 128, NH), f32)
    for h in range(NH):
        heye[h, :, h] = 1.0

    common = dict(
        wqa=np.ascontiguousarray(L0_attn_Wqa, f32),
        wqb=wqb_p,
        wkva=np.ascontiguousarray(L0_attn_Wkva, f32),
        wkvb=wkvb_p,
        wo=np.ascontiguousarray(L0_attn_Wo, f32),
        wg_t=wg_t, wu_t=wu_t, wd_t=wd_t,
        wlnq=np.ascontiguousarray(L0_attn_q_a_ln_w, f32),
        wlnkv=np.ascontiguousarray(L0_attn_kv_a_ln_w, f32),
        cosk=cos, srotk=srot,
        idf=idf, idb=idb, onescb=onescb, onesrf=onesrf, heye=heye,
    )

    in_maps = []
    for c in range(8):
        b, p = c // 2, c % 2
        own_pos = np.arange(S).reshape(16, 256)[p::2].reshape(-1)
        m = dict(common)
        m["x_full"] = x[b]
        m["x_own"] = np.ascontiguousarray(x[b][own_pos])
        m["cosq8"] = np.ascontiguousarray(np.tile(cos[own_pos], (1, 8)))
        m["srotq8"] = np.ascontiguousarray(np.tile(srot[own_pos], (1, 8)))
        kk = np.arange(128)[None, :, None]
        ccol = np.arange(512)[None, None, :]
        rr = np.arange(8)[:, None, None]
        cc2 = ccol + 256 * (ccol >= 256)
        valid = (128 * rr + kk) <= (256 * p + cc2)
        m["masks"] = np.where(valid, 0.0, NEG).astype(bfdt)
        in_maps.append(m)

    nc = build_nc()
    res = run_bass_kernel_spmd(nc, in_maps, core_ids=list(range(8)))
    out = np.empty((B, S, H), f32)
    for c in range(8):
        b, p = c // 2, c % 2
        out[b].reshape(16, 256, H)[p::2] = res.results[c]["y"].reshape(8, 256, H)
    return out
